# version 06
# baseline (speedup 1.0000x reference)
import os
import contextlib
import numpy as np
import concourse.bass as bass
import concourse.mybir as mybir
from concourse.bass_utils import run_bass_kernel_spmd

F32 = mybir.dt.float32
BF16 = mybir.dt.bfloat16
I32 = mybir.dt.int32
AF = mybir.ActivationFunctionType
ALU = mybir.AluOpType
AX = mybir.AxisListType

D = 1024
KD = 8
TC = 2048
WIN = 3200
ISDH = 0.08838834764831845
EPS = 1e-5
DEF_STAGE = 9
ENGS = ("tensor", "vector", "scalar", "gpsimd", "sync")


class MK:
    def __init__(self, nc, stage):
        self.nc = nc
        self.stage = stage
        self.ctx = contextlib.ExitStack()
        self.prog = {e: [] for e in ENGS}
        self.sems = {}
        self.cnt = {}
        self.nbar = 0

    SEM_ALIAS = {
        "fl_c": "v_lc", "fl_m": "t_lm", "fl_s": "s_lm", "fl_b": "t_lb",
        "fl_d": "v_ls", "fl_q": "s_lq", "fl_v": "t_lv", "fl_e": "v_lv",
        "fl_sq": "s_lv", "fl_i": "v_li", "fl_ib": "t_li", "fl_h": "v_ln",
        "l6_c": "v_lc", "l6_m": "t_lm", "l6_s": "s_lm", "l6_b": "t_lb",
        "l6_d": "v_ls", "l6_q": "s_lq", "l6_v": "t_lv", "l6_e": "v_lv",
        "l6_sq": "s_lv", "l6_i": "v_li", "l6_ib": "t_li", "l6_h": "v_ln",
        "t5_p": "t_pj", "v5_p": "s_pj",
        "t6_f1": "t_f1", "s6_f1": "s_f1", "t6_f2": "t_f2", "v6_f2": "v_f2",
        "t5_o": "t_o", "v5_o": "s_o", "t_o2": "t_o", "v_o2": "s_o",
        "t_ot": "t_xi", "v_ot": "v_xi",
        "t5_bb": "t_g", "v5_bb": "v_g", "t6_gb": "t_g", "s6_gb": "v_g",
        "t_db": "t_g", "t6_r": "t_g", "t6_st": "t_g", "v6_st": "v_g",
        "t6_g": "t_g", "v6_g": "v_g", "t_m3": "t_g", "v_m3": "v_g",
        "v_m2": "v_g", "t_rb": "t_g", "s6_rs": "s_lm", "v6_sm": "v_g",
        "s5_x": "s_x", "s6_x": "s_x", "s6_a": "s_x", "v5_s": "v_g",
        "v5_dn": "v_g", "s5_l": "s_lq", "v5_ng": "v_g", "s5_e": "s_x",
        "v6_sm": "v_g", "s6_l": "s_lq", "v6_ng": "v_g", "s6_e": "s_x",
        "v5_x": "v_g", "v5_pv": "v_sq", "v6_mg": "v_mg", "f_bh": "v_g",
        "f_p": "wl", "y_fq": "wl", "y5_b": "wl", "y6_p": "wl",
        "y6_qw": "wl", "y6_wt": "wl", "c0": "wl", "xbf": "xtt",
        "y_fc": "y_at", "y6_hc": "y_at", "y_in": "y_at", "y_qa": "y_at",
        "y_xr": "y_at", "y_at2": "y_at", "y_l0": "y_at",
        "y_fh0": "y_p0", "y_fh1": "y_p1", "y6_h0": "y_p0", "y6_qo": "y_p1",
        "y_ot0": "y_o0", "y_ot1": "y_o1",
        "g5_w": "wlg",
        "v6_q": "v_dq", "v6_t": "v_g", "s6_g": "s_lq", "y_fs": "xtt",
        "y_fq1": "y_o0", "y6_qw2": "y_o0", "y6_qo2": "y_o1",
    }

    def sem(self, name):
        name = self.SEM_ALIAS.get(name, name)
        if name not in self.sems:
            self.sems[name] = self.ctx.enter_context(self.nc.semaphore(name))
            self.cnt[name] = 0
        return self.sems[name]

    def op(self, eng, build, sem=None, n=1):
        if sem is not None:
            sem = self.SEM_ALIAS.get(sem, sem)
            s = self.sem(sem)
            self.cnt[sem] += n
            self.prog[eng].append(lambda e, b=build, s=s, n=n: b(e).then_inc(s, n))
        else:
            self.prog[eng].append(build)

    def wait(self, eng, sem, val=None):
        sem = self.SEM_ALIAS.get(sem, sem)
        s = self.sem(sem)
        v = self.cnt[sem] if val is None else val
        self.prog[eng].append(lambda e, s=s, v=v: e.wait_ge(s, v))

    def barrier(self, engs=ENGS):
        self.nbar += 1
        s = self.sem("bar")
        self.cnt["bar"] += len(engs)
        tgt = self.cnt["bar"]
        for eng in engs:
            self.prog[eng].append(lambda e, s=s: e.nop().then_inc(s, 1))
            self.prog[eng].append(lambda e, s=s, t=tgt: e.wait_ge(s, t))

    def gc(self, name, d=0):
        name = self.SEM_ALIAS.get(name, name)
        return self.cnt.get(name, d)

    def sb(self, name, shape, dtype, scope=None):
        return (scope or self.ctx).enter_context(self.nc.sbuf_tensor(name, shape, dtype))

    def build(self):
        nc = self.nc
        dt_ = nc.dram_tensor
        S = self.stage
        self.xwin = dt_("xwin", [WIN, D], F32, kind="ExternalInput").ap()
        self.xres = dt_("xres", [TC, D], F32, kind="ExternalInput").ap()
        self.mv = dt_("mv", [128, 1], F32, kind="ExternalInput").ap()
        self.mv1 = dt_("mv1", [128, 1], F32, kind="ExternalInput").ap()
        self.winT = [dt_(f"winT{l}", [D, 3 * D], F32, kind="ExternalInput").ap() for l in range(3)]
        self.binb = [dt_(f"bin{l}", [128, 24], F32, kind="ExternalInput").ap() for l in range(3)]
        self.binv = [dt_(f"binv{l}", [1, D], F32, kind="ExternalInput").ap() for l in range(3)]
        self.woutT = [dt_(f"woutT{l}", [D, D], F32, kind="ExternalInput").ap() for l in range(3)]
        self.boutb = [dt_(f"bout{l}", [128, 8], F32, kind="ExternalInput").ap() for l in range(3)]
        self.ffg = dt_("ffg", [128, 8], F32, kind="ExternalInput").ap()
        self.ffb = dt_("ffb", [128, 8], F32, kind="ExternalInput").ap()
        self.fc1q = dt_("fc1q", [D, 4096], I32, kind="ExternalInput").ap()
        self.fc1s = dt_("fc1s", [8, 4096], F32, kind="ExternalInput").ap()
        self.fc1b = dt_("fc1b", [128, 32], F32, kind="ExternalInput").ap()
        self.fc2q = dt_("fc2q", [4096, D], I32, kind="ExternalInput").ap()
        self.fc2s = dt_("fc2s", [32, D], F32, kind="ExternalInput").ap()
        self.fc2b = dt_("fc2b", [128, 8], F32, kind="ExternalInput").ap()
        self.wq, self.wk, self.wv, self.wo = [
            dt_(n, [D, D], F32, kind="ExternalInput").ap() for n in ("wq", "wk", "wv", "wo")]
        self.bq, self.bk, self.bv = [
            dt_(n, [1, D], F32, kind="ExternalInput").ap() for n in ("bq", "bk", "bv")]
        self.bo = dt_("bo", [128, 8], F32, kind="ExternalInput").ap()
        self.ew1 = dt_("ew1", [9, 2048, D], F32, kind="ExternalInput").ap()
        self.ew2 = dt_("ew2", [9, D, 2048], F32, kind="ExternalInput").ap()
        self.rwT = dt_("rwT", [D, 8], F32, kind="ExternalInput").ap()
        self.g1 = dt_("g1", [128, 8], F32, kind="ExternalInput").ap()
        self.b1 = dt_("b1", [128, 8], F32, kind="ExternalInput").ap()
        self.ones128 = dt_("ones128", [1, 128], F32, kind="ExternalInput").ap()
        self.onescol_bf = dt_("onescol_bf", [128, 1], BF16, kind="ExternalInput").ap()
        self.on1024_bf = dt_("on1024_bf", [128, 1], BF16, kind="ExternalInput").ap()
        self.idn_bf = dt_("idn_bf", [128, 128], BF16, kind="ExternalInput").ap()
        self.idn_f = dt_("idn_f", [128, 128], F32, kind="ExternalInput").ap()
        self.sel8 = dt_("sel8", [8, 1024], F32, kind="ExternalInput").ap()
        self.sel8h = dt_("sel8h", [128, 64], BF16, kind="ExternalInput").ap()
        self.pw2 = dt_("pw2", [128, 8], F32, kind="ExternalInput").ap()
        self.out = dt_("out", [TC, D], F32, kind="ExternalOutput").ap()
        self.xtm_scr = dt_("xtm_scr", [WIN, D], BF16, kind="Internal").ap()
        self.xbf_scr = dt_("xbf_scr", [D, WIN], BF16, kind="Internal").ap()
        self.l0o_scr = dt_("l0o_scr", [D, 3584], BF16, kind="Internal").ap()
        self.l1o_scr = dt_("l1o_scr", [D, 3072], BF16, kind="Internal").ap()
        self.q_scr = dt_("q_scr", [D, TC], BF16, kind="Internal").ap()
        self.k_scr = dt_("k_scr", [D, TC], BF16, kind="Internal").ap()
        self.v_scr = dt_("v_scr", [TC, D], BF16, kind="Internal").ap()
        self.attn_scr = dt_("attn_scr", [D, TC], BF16, kind="Internal").ap()
        self.h_scr = dt_("h_scr", [D, TC], BF16, kind="Internal").ap()
        self.w1q_scr = dt_("w1q_scr", [2, 2048, D], BF16, kind="Internal").ap()
        self.w2q_scr = dt_("w2q_scr", [2, D, 2048], BF16, kind="Internal").ap()

        self.ones_sb = self.sb("ones_sb", [1, 128], F32)
        self.onescol_sb = self.sb("onescol_sb", [128, 1], BF16)
        self.on1024_sb = self.sb("on1024_sb", [128, 1], BF16)
        self.idnbf_sb = self.sb("idnbf_sb", [128, 128], BF16)
        self.idnf_sb = self.sb("idnf_sb", [128, 128], F32)
        self.mv_sb = self.sb("mv_sb", [128, 1], F32)
        self.sel8_sb = self.sb("sel8_sb", [8, 1024], F32)
        self.sel8h_sb = self.sb("sel8h_sb", [128, 64], BF16)
        self.pw2_sb = self.sb("pw2_sb", [128, 8], F32)
        self.mv1_sb = self.sb("mv1_sb", [128, 1], F32)
        self.row_sb = self.sb("row_sb", [1, 2048], F32)
        self.lnbf = self.sb("lnbf", [128, KD, 512], BF16)
        self.lnbf2 = self.lnbf
        self.scal8 = self.sb("scal8", [128, 8], F32)
        self.mu_sb = self.sb("mu_sb", [1, 512], F32)
        self.std_sb = self.sb("std_sb", [1, 512], F32)
        self.istd_sb = self.sb("istd_sb", [1, 512], F32)

        self.ps_a = self.ctx.enter_context(nc.psum_tensor("ps_a", [128, 2, 512], F32))
        self.ps_b = self.ctx.enter_context(nc.psum_tensor("ps_b", [128, 2, 512], F32))
        self.ps_c = self.ctx.enter_context(nc.psum_tensor("ps_c", [128, 2, 512], F32))
        self.ps_d = self.ctx.enter_context(nc.psum_tensor("ps_d", [128, 2, 512], F32))

        self.p0_consts_x()
        with contextlib.ExitStack() as attn_scope:
            self.alloc_attn(attn_scope)
            self.attn_level(0)
            self.barrier()
            self.attn_level(1)
            self.barrier()
            self.attn_level(2)
            self.barrier()
        self.stream = self.sb("stream", [128, KD, TC], F32)
        with contextlib.ExitStack() as sc:
            self.p3b_outproj_l2(sc)
            self.barrier()
        self.ln_module()
        if S >= 2:
            self.barrier()
            with contextlib.ExitStack() as sc:
                self.p4_l0ffn(sc)
                self.barrier()
            self.ln_module()
        if S >= 3:
            self.barrier()
            with contextlib.ExitStack() as sc:
                self.p5_l1attn(sc)
                self.barrier()
            self.ln_module()
        if S >= 4:
            self.barrier()
            with contextlib.ExitStack() as sc:
                self.p6_l1ffn(sc)
                self.barrier()
            self.ln_module()
        self.barrier()
        self.p7_store()

        blk = self.ctx.enter_context(nc.Block())
        for name in ENGS:
            lst = self.prog[name]
            getattr(blk, name)(lambda e, lst=lst: [f(e) for f in lst])
        self.ctx.close()
        return nc

    def p0_consts_x(self):
        p = self
        for dst_, src_ in (
            (p.ones_sb, p.ones128), (p.onescol_sb, p.onescol_bf),
            (p.on1024_sb, p.on1024_bf), (p.idnbf_sb, p.idn_bf),
            (p.idnf_sb, p.idn_f), (p.mv_sb, p.mv), (p.sel8_sb, p.sel8),
            (p.sel8h_sb, p.sel8h), (p.pw2_sb, p.pw2),
            (p.mv1_sb, p.mv1),
        ):
            self.op("sync", lambda e, d=dst_, s=src_: e.dma_start(d[:, :], s), "c0", 16)
        self.op("gpsimd", lambda e: e.dma_start(p.xtm_scr, p.xwin), "xtm", 16)
        self.wait("sync", "xtm")
        xstg = self.lnbf.ap().bitcast(BF16).rearrange("p a f -> p (a f)")
        for dtl in range(KD):
            self.op("sync", lambda e, d=dtl: e.dma_start(
                xstg[:, 0:WIN], p.xtm_scr[:, d * 128:(d + 1) * 128], transpose=True), "xtt", 16)
            self.wait("sync", "xtt")
            self.op("sync", lambda e, d=dtl: e.dma_start(
                p.xbf_scr[d * 128:(d + 1) * 128, :], xstg[:, 0:WIN]), "xbf", 16)
            self.wait("sync", "xbf")
        self.wait("sync", "c0")
        self.barrier(("sync", "gpsimd", "vector", "scalar", "tensor"))

    def alloc_attn(self, sc):
        p = self
        self.winT_sb = self.sb("winT_sb", [128, KD, 3 * D], BF16, sc)
        self.woutT_sb = self.sb("woutT_sb", [128, KD, D], BF16, sc)
        self.binT_sb = self.sb("binT_sb", [128, 24], F32, sc)
        self.boutT_sb = self.sb("boutT_sb", [128, 8], F32, sc)
        self.vb_bc = self.sb("vb_bc", [128, D], F32, sc)
        self.inb = self.sb("inb", [128, 2, KD, 512], BF16, sc)
        self.pstage = self.sb("pstage", [128, 2, KD, 512], BF16, sc)
        self.q_at = self.sb("q_at", [128, 2, 512], BF16, sc)
        self.k_at = self.sb("k_at", [128, 2, 16, 128], BF16, sc)
        self.v_at = self.sb("v_at", [128, 2, 16, 128], BF16, sc)
        self.Pp = self.sb("Pp", [128, 2, 512], BF16, sc)
        self.attnU = self.sb("attnU", [128, 8, 512], BF16, sc)
        self.attnN = self.sb("attnN", [128, 2, 8, 512], BF16, sc)
        self.sqs = self.sb("sqs", [128, 2, 512], BF16, sc)
        self.m2q_sb = self.sb("m2q_sb", [1, 8, 4], F32, sc)
        self.m2k_sb = self.sb("m2k_sb", [1, 8, 4], F32, sc)
        self.m2_sb = self.sb("m2_sb", [1, 4, 8], F32, sc)
        self.negm_row = self.sb("negm_row", [1, 8], F32, sc)
        self.negm_sb = self.sb("negm_sb", [128, 8], F32, sc)
        self.rden_sb = self.sb("rden_sb", [8, 512], F32, sc)
        self.aT_st = self.sb("aT_st", [128, 2, KD, 512], BF16, sc)

    def level_weights(self, l):
        p = self
        self.op("gpsimd", lambda e: e.dma_start(
            p.winT_sb[:, :, :], p.winT[l].rearrange("(a p) f -> p a f", p=128)), "wlg", 16)
        self.op("gpsimd", lambda e: e.dma_start(
            p.woutT_sb[:, :, :], p.woutT[l].rearrange("(a p) f -> p a f", p=128)), "wlg", 16)
        self.op("sync", lambda e: e.dma_start(p.binT_sb[:, :], p.binb[l]), "wl", 16)
        self.op("sync", lambda e: e.dma_start(p.boutT_sb[:, :], p.boutb[l]), "wl", 16)
        self.op("sync", lambda e: e.dma_start(p.row_sb[0:1, 0:D], p.binv[l]), "wl", 16)
        self.wait("tensor", "wl")
        self.wait("tensor", "wlg")
        for n in range(2):
            self.op("tensor", lambda e, n=n: e.matmul(
                p.ps_c[:, n, :], p.ones_sb[0:1, :], p.row_sb[0:1, n * 512:(n + 1) * 512],
                start=True, stop=True), "t_g" if n == 1 else None, 1)
        self.wait("vector", "t_g")
        for n in range(2):
            self.op("vector", lambda e, n=n: e.tensor_copy(
                p.vb_bc[:, n * 512:(n + 1) * 512], p.ps_c[:, n, :]), "v_g" if n == 1 else None, 1)

    def chunk_input(self, l, j, T):
        pass

    def sub_input(self, l, j, s_i):
        p = self
        par = s_i % 2
        if l == 0:
            src_ = p.xbf_scr
            oa, ob = 448 * j + s_i * 512, None
        elif l == 1:
            src_ = p.l0o_scr
            oa, ob = 960 * j + s_i * 512, 960 * j + 448 + s_i * 512
        else:
            src_ = p.l1o_scr
            oa, ob = s_i * 512, 960 + s_i * 512
        self.wait("sync", "t_pj", self.gc("t_pj"))
        self.op("sync", lambda e, oa=oa, par=par, src_=src_: e.dma_start(
            p.inb[:, par, :, :] if True else None,
            src_.rearrange("(a p) f -> p a f", p=128)[:, :, oa:oa + 512]), "y_in", 16)
        if ob is not None:
            self.op("sync", lambda e, ob=ob, par=par, src_=src_: e.dma_start(
                p.aT_st[:, par, :, :],
                src_.rearrange("(a p) f -> p a f", p=128)[:, :, ob:ob + 512]), "y_in", 16)
            self.wait("vector", "y_in")
            self.op("vector", lambda e, par=par: e.tensor_scalar(
                p.inb[:, par, :, :], p.inb[:, par, :, :], p.mv1_sb[:, 0:1], None,
                ALU.mult), None)
            self.op("vector", lambda e, par=par: e.scalar_tensor_tensor(
                p.inb[:, par, :, :], p.aT_st[:, par, :, :], p.mv_sb[:, 0:1],
                p.inb[:, par, :, :], ALU.mult, ALU.add), "v_mg", 1)
        else:
            self.wait("vector", "y_in")
            self.op("vector", lambda e: e.nop(), "v_mg", 1)

    def chunk_qkv(self, l, j, T):
        p = self
        NS = T // 512
        for s_i in range(NS):
            par = s_i % 2
            self.sub_input(l, j, s_i)
            self.wait("tensor", "v_mg")
            self.wait("tensor", "v_g")
            for qk in range(2):
                for h in range(8):
                    m = qk * 8 + h
                    g = self.gc("t_pj")
                    if g >= 2:
                        self.wait("tensor", "s_pj", g - 1)
                    gp = g % 2
                    for k in range(KD):
                        self.op("tensor", lambda e, m=m, k=k, par=par, gp=gp: e.matmul(
                            p.ps_a[:, gp, :], p.winT_sb[:, k, m * 128:(m + 1) * 128],
                            p.inb[:, par, k, :], start=(k == 0), stop=(k == KD - 1)),
                            "t_pj" if k == KD - 1 else None, 1)
                    self.wait("scalar", "t_pj")
                    if h == 0:
                        self.wait("scalar", "y_p%d" % par, None)
                    self.op("scalar", lambda e, h=h, m=m, gp=gp, par=par: e.activation(
                        p.pstage[:, par, h, :], p.ps_a[:, gp, :], AF.Identity,
                        bias=p.binT_sb[:, m:m + 1]), "s_pj", 1)
                    self.qknorm_tile(h, s_i, from_q=(qk == 0))
                self.wait("sync", "s_pj")
                self.wait("sync", "v_sq")
                dscr = p.q_scr if qk == 0 else p.k_scr
                self.op("sync", lambda e, s_i=s_i, par=par, dscr=dscr: e.dma_start(
                    dscr.rearrange("(a p) f -> p a f", p=128)[:, :, s_i * 512:(s_i + 1) * 512],
                    p.pstage[:, par, :, :]), "y_p%d" % par, 16)
            for tt in range(4):
                for n in range(2):
                    g = self.gc("t_pj")
                    if g >= 2:
                        self.wait("tensor", "s_pj", g - 1)
                    gp = g % 2
                    for k in range(KD):
                        self.op("tensor", lambda e, tt=tt, n=n, k=k, par=par, gp=gp: e.matmul(
                            p.ps_a[:, gp, :], p.inb[:, par, k, tt * 128:(tt + 1) * 128],
                            p.winT_sb[:, k, 2 * D + n * 512:2 * D + (n + 1) * 512],
                            start=(k == 0), stop=(k == KD - 1)), "t_pj" if k == KD - 1 else None, 1)
                    self.wait("vector", "t_pj")
                    if tt == 0 and n == 0:
                        self.wait("vector", "y_p%d" % par, None)
                    pst_v = p.pstage[:, :, :, :].rearrange("q a b f -> q a (b f)")
                    self.op("vector", lambda e, tt=tt, n=n, gp=gp, par=par, pst_v=pst_v: e.tensor_tensor(
                        pst_v[:, par, tt * 1024 + n * 512: tt * 1024 + (n + 1) * 512],
                        p.ps_a[:, gp, :], p.vb_bc[:, n * 512:(n + 1) * 512], ALU.add), "s_pj", 1)
            self.wait("sync", "s_pj")
            pst_v = p.pstage[:, :, :, :].rearrange("q a b f -> q a (b f)")
            self.op("sync", lambda e, s_i=s_i, par=par, pst_v=pst_v: e.dma_start(
                p.v_scr[s_i * 512:(s_i + 1) * 512, :].rearrange("(a p) f -> p a f", p=128),
                pst_v[:, par, :].rearrange("p (a f) -> p a f", a=4)), "y_p%d" % par, 16)
        self.norm_final(T)

    def qknorm_tile(self, h, s_i, from_q):
        p = self
        par = s_i % 2
        sp = self.gc("v_sq") % 2
        self.wait("vector", "s_pj")
        self.wait("vector", "t_n2", self.gc("t_n2"))
        self.op("vector", lambda e, h=h, par=par, sp=sp: e.tensor_tensor(
            p.sqs[:, sp, :], p.pstage[:, par, h, :], p.pstage[:, par, h, :],
            ALU.mult), "v_sq", 1)
        self.wait("tensor", "v_sq")
        npar = self.gc("t_n2") % 2
        self.wait("tensor", "v_n2", max(0, self.gc("v_n2") - 1))
        self.op("tensor", lambda e, sp=sp, npar=npar: e.matmul(
            p.ps_d[0:1, npar, :], p.onescol_sb[:, 0:1], p.sqs[:, sp, :],
            start=True, stop=True), "t_n2", 1)
        self.wait("vector", "t_n2")
        dst = p.m2q_sb if from_q else p.m2k_sb
        self.op("vector", lambda e, h=h, s_i=s_i, npar=npar, dst=dst: e.tensor_reduce(
            dst[0:1, h, s_i:s_i + 1], p.ps_d[0:1, npar, :], AX.X, ALU.max), "v_n2", 1)

    def norm_final(self, T):
        p = self
        NS = T // 512
        self.op("vector", lambda e, NS=NS: e.tensor_reduce(
            p.m2_sb[0:1, 0, :], p.m2q_sb[0:1, :, 0:NS], AX.X, ALU.max), None)
        self.op("vector", lambda e, NS=NS: e.tensor_reduce(
            p.m2_sb[0:1, 1, :], p.m2k_sb[0:1, :, 0:NS], AX.X, ALU.max), None)
        self.op("vector", lambda e: e.tensor_tensor(
            p.m2_sb[0:1, 2, :], p.m2_sb[0:1, 0, :], p.m2_sb[0:1, 1, :], ALU.add), None)
        self.op("vector", lambda e: e.tensor_scalar(
            p.negm_row[0:1, 0:8], p.m2_sb[0:1, 2, :], -0.5 * ISDH, None,
            ALU.mult), "v_m2", 1)
        self.wait("tensor", "v_m2")
        self.op("tensor", lambda e: e.matmul(
            p.ps_c[:, 1, 0:8], p.ones_sb[0:1, :], p.negm_row[0:1, 0:8],
            start=True, stop=True), "t_m3", 1)
        self.wait("vector", "t_m3")
        self.op("vector", lambda e: e.tensor_copy(p.negm_sb[:, 0:8], p.ps_c[:, 1, 0:8]), "v_m3", 1)

    def chunk_attention(self, l, j, T):
        p = self
        NQC = T // 512
        NKT = T // 128
        self.wait("tensor", "v_m3")
        self.wait("sync", "y_p0")
        self.wait("sync", "y_p1")
        for qc in range(NQC):
            qpar = qc % 2
            if qc >= 2:
                self.wait("tensor", "v_rc", self.gc("v_rc"))
            van_base = self.gc("v_an")
            for h in range(8):
                lpar = (qc * 8 + h) % 2
                if qc * 8 + h >= 2:
                    self.wait("sync", "t_pv", self.gc("t_pv"))
                self.op("sync", lambda e, h=h, qc=qc, lpar=lpar: e.dma_start(
                    p.q_at[:, lpar, :],
                    p.q_scr[h * 128:(h + 1) * 128, qc * 512:(qc + 1) * 512]), "y_qa", 16)
                self.op("sync", lambda e, h=h, lpar=lpar, NKT=NKT: e.dma_start(
                    p.k_at[:, lpar, 0:NKT, :],
                    p.k_scr[h * 128:(h + 1) * 128, 0:NKT * 128].rearrange(
                        "p (a f) -> p a f", f=128)), "y_qa", 16)
                self.op("sync", lambda e, h=h, lpar=lpar, NKT=NKT: e.dma_start(
                    p.v_at[:, lpar, 0:NKT, :],
                    p.v_scr[0:NKT * 128, h * 128:(h + 1) * 128].rearrange(
                        "(a p) f -> p a f", p=128)), "y_qa", 16)
                self.wait("tensor", "y_qa")
                g_pv = self.gc("g_pv")
                if g_pv >= 2:
                    self.wait("tensor", "s_u", g_pv - 1)
                self.cnt["g_pv"] = g_pv + 1
                def s_mm(kt):
                    spar = kt % 2
                    self.op("tensor", lambda e, kt=kt, spar=spar, lpar=lpar: e.matmul(
                        p.ps_a[:, spar, :], p.k_at[:, lpar, kt, :],
                        p.q_at[:, lpar, :], start=True, stop=True), "t_s", 1)
                    self.wait("scalar", "t_s")
                    if kt >= 2:
                        self.wait("scalar", "t_pv", self.gc("t_pv") - 1)
                    self.op("scalar", lambda e, h=h, spar=spar: e.activation(
                        p.Pp[:, spar, :], p.ps_a[:, spar, :], AF.Exp,
                        bias=p.negm_sb[:, h:h + 1], scale=ISDH), "s_x", 1)

                def dnpv_mm(kt):
                    spar = kt % 2
                    self.wait("tensor", "s_x", self.sx_base + kt + 1)
                    self.op("tensor", lambda e, h=h, kt=kt, spar=spar, qpar=qpar, NKT=NKT: e.matmul(
                        p.ps_d[0:8, qpar, :], p.sel8h_sb[:, h * 8:(h + 1) * 8],
                        p.Pp[:, spar, :], start=(h == 0 and kt == 0),
                        stop=(h == 7 and kt == NKT - 1)),
                        "t_dn" if (h == 7 and kt == NKT - 1) else None, 1)
                    self.op("tensor", lambda e, kt=kt, spar=spar, lpar=lpar, pvpar=g_pv % 2, NKT=NKT: e.matmul(
                        p.ps_b[:, pvpar, :], p.v_at[:, lpar, kt, :],
                        p.Pp[:, spar, :], start=(kt == 0), stop=(kt == NKT - 1)), "t_pv", 1)

                self.sx_base = self.gc("s_x")
                s_mm(0)
                for kt in range(1, NKT):
                    s_mm(kt)
                    dnpv_mm(kt - 1)
                dnpv_mm(NKT - 1)
                self.wait("scalar", "t_pv")
                self.wait("scalar", "v_an", van_base)
                self.op("scalar", lambda e, h=h, pvpar=g_pv % 2: e.activation(
                    p.attnU[:, h, :], p.ps_b[:, pvpar, :], AF.Copy), "s_u", 1)
            self.wait("vector", "t_dn")
            self.op("vector", lambda e, qpar=qpar: e.reciprocal(
                p.rden_sb[:, :], p.ps_d[0:8, qpar, :]), "v_rc", 1)
            for h in range(8):
                bp = h % 2
                self.wait("tensor", "v_rc")
                if h >= 2:
                    self.wait("tensor", "v_an", self.gc("v_an"))
                else:
                    self.wait("tensor", "v_an", van_base)
                self.op("tensor", lambda e, h=h, bp=bp: e.matmul(
                    p.ps_c[:, bp, :], p.sel8_sb[:, h * 128:(h + 1) * 128],
                    p.rden_sb[:, :], start=True, stop=True), "t_rb", 1)
                self.wait("vector", "t_rb")
                self.wait("vector", "s_u")
                if qc >= 2 and h == 0:
                    self.wait("vector", "y_an%d" % qpar, None)
                self.op("vector", lambda e, h=h, bp=bp, qpar=qpar: e.tensor_tensor(
                    p.attnN[:, qpar, h, :], p.attnU[:, h, :], p.ps_c[:, bp, :],
                    ALU.mult), "v_an", 1)
            self.wait("sync", "v_an")
            self.op("sync", lambda e, qc=qc, qpar=qpar: e.dma_start(
                p.attn_scr.rearrange("(a p) f -> p a f", p=128)[:, :, qc * 512:(qc + 1) * 512],
                p.attnN[:, qpar, :, :]), "y_an%d" % qpar, 16)

    def chunk_outproj(self, l, j, T):
        p = self
        NS = T // 512
        dst = p.l0o_scr if l == 0 else p.l1o_scr
        base = 512 * j * 1 if l == 0 else 1024 * j
        for s_i in range(NS):
            apar = s_i % 2
            self.wait("sync", "y_an0")
            self.wait("sync", "y_an1")
            if s_i >= 2:
                self.wait("sync", "t_o", self.gc("t_o"))
            self.op("sync", lambda e, s_i=s_i, apar=apar: e.dma_start(
                p.aT_st[:, apar, :, :],
                p.attn_scr.rearrange("(a p) f -> p a f", p=128)[:, :, s_i * 512:(s_i + 1) * 512]),
                "y_at", 16)
            self.wait("tensor", "y_at")
            for dtl in range(8):
                g = self.gc("t_o")
                if g >= 2:
                    self.wait("tensor", "s_o", g - 1)
                par = g % 2
                for k in range(KD):
                    self.op("tensor", lambda e, dtl=dtl, k=k, apar=apar, par=par: e.matmul(
                        p.ps_a[:, par, :], p.woutT_sb[:, k, dtl * 128:(dtl + 1) * 128],
                        p.aT_st[:, apar, k, :], start=(k == 0), stop=(k == KD - 1)),
                        "t_o" if k == KD - 1 else None, 1)
                self.wait("scalar", "t_o")
                if self.gc("s_o") >= 2 and dtl == 0:
                    self.wait("scalar", "y_o%d" % apar, None)
                self.op("scalar", lambda e, dtl=dtl, apar=apar, par=par: e.activation(
                    p.attnN[:, apar, dtl, :], p.ps_a[:, par, :], AF.Identity,
                    bias=p.boutT_sb[:, dtl:dtl + 1]), "s_o", 1)
            self.wait("sync", "s_o")
            self.op("sync", lambda e, s_i=s_i, apar=apar, dst=dst, base=base: e.dma_start(
                dst.rearrange("(a p) f -> p a f", p=128)[:, :, base + s_i * 512:base + (s_i + 1) * 512],
                p.attnN[:, apar, :, :]), "y_o%d" % apar, 16)
        if l == 0:
            self.op("sync", lambda e: e.nop(), "y_l0", 1)

    def attn_level(self, l):
        T = (512, 1024, 2048)[l]
        nch = (7, 3, 1)[l]
        self.level_weights(l)
        for j in range(nch):
            self.chunk_qkv(l, j, T)
            self.chunk_attention(l, j, T)
            if l < 2:
                self.chunk_outproj(l, j, T)
            self.barrier()

    def p3b_outproj_l2(self, sc):
        p = self
        self.woutT2s = self.sb("woutT2_sb", [128, KD, D], BF16, sc)
        self.boutT2s = self.sb("boutT2_sb", [128, 8], F32, sc)
        self.aT2_st = self.sb("aT2_st", [128, 2, KD, 512], BF16, sc)
        self.xrs_tm = self.sb("xrs_tm", [128, 2, D], F32, sc)
        self.op("gpsimd", lambda e: e.dma_start(
            p.woutT2s[:, :, :], p.woutT[2].rearrange("(a p) f -> p a f", p=128)), "wlg", 16)
        self.op("sync", lambda e: e.dma_start(p.boutT2s[:, :], p.boutb[2]), "wl", 16)
        for tt in range(16):
            par = tt % 2
            if tt >= 2:
                self.wait("sync", "v_xi", self.gc("v_xi"))
            self.op("sync", lambda e, tt=tt, par=par: e.dma_start(
                p.xrs_tm[:, par, :], p.xres[tt * 128:(tt + 1) * 128, :]), "y_xr", 16)
            self.wait("tensor", "y_xr")
            for dtl in range(8):
                bp = (tt * 8 + dtl) % 2
                if tt * 8 + dtl >= 2:
                    self.wait("tensor", "v_xi", self.gc("v_xi") - 1)
                self.op("tensor", lambda e, tt=tt, dtl=dtl, par=par, bp=bp: e.transpose(
                    p.ps_b[:, bp, 0:128], p.xrs_tm[:, par, dtl * 128:(dtl + 1) * 128],
                    p.idnf_sb[:, :]), "t_xi", 1)
                self.wait("vector", "t_xi")
                self.op("vector", lambda e, tt=tt, dtl=dtl, bp=bp: e.tensor_copy(
                    p.stream[:, dtl, tt * 128:(tt + 1) * 128], p.ps_b[:, bp, 0:128]), "v_xi", 1)
        self.wait("tensor", "wl")
        self.wait("tensor", "wlg")
        for s_i in range(4):
            apar = s_i % 2
            if s_i >= 2:
                self.wait("sync", "t_o2", self.gc("t_o2"))
            self.op("sync", lambda e, s_i=s_i, apar=apar: e.dma_start(
                p.aT2_st[:, apar, :, :],
                p.attn_scr.rearrange("(a p) f -> p a f", p=128)[:, :, s_i * 512:(s_i + 1) * 512]),
                "y_at2", 16)
            self.wait("tensor", "y_at2")
            self.wait("tensor", "v_xi")
            for dtl in range(8):
                g = self.gc("t_o2")
                if g >= 2:
                    self.wait("tensor", "v_o2", g - 1)
                par = g % 2
                for k in range(KD):
                    self.op("tensor", lambda e, dtl=dtl, k=k, apar=apar, par=par: e.matmul(
                        p.ps_a[:, par, :], p.woutT2s[:, k, dtl * 128:(dtl + 1) * 128],
                        p.aT2_st[:, apar, k, :], start=(k == 0), stop=(k == KD - 1)),
                        "t_o2" if k == KD - 1 else None, 1)
                self.wait("vector", "t_o2")
                self.op("vector", lambda e, dtl=dtl, s_i=s_i, par=par: e.scalar_tensor_tensor(
                    p.stream[:, dtl, s_i * 512:(s_i + 1) * 512], p.ps_a[:, par, :],
                    p.boutT2s[:, dtl:dtl + 1],
                    p.stream[:, dtl, s_i * 512:(s_i + 1) * 512], ALU.add, ALU.add), "v_o2", 1)

    def ln_module(self):
        p = self
        for c in range(4):
            cs = slice(c * 512, (c + 1) * 512)
            par = c % 2
            for k in range(KD):
                self.op("vector", lambda e, k=k, cs=cs: e.tensor_copy(
                    p.lnbf[:, k, :], p.stream[:, k, cs]), "v_lc" if k == KD - 1 else None, 1)
            self.wait("tensor", "v_lc")
            if c >= 2:
                self.wait("tensor", "v_ls", self.gc("v_ls"))
            for k in range(KD):
                self.op("tensor", lambda e, k=k, par=par: e.matmul(
                    p.ps_d[0:1, par, :], p.on1024_sb[:, 0:1], p.lnbf[:, k, :],
                    start=(k == 0), stop=(k == KD - 1)), "t_lm" if k == KD - 1 else None, 1)
            self.wait("scalar", "t_lm")
            self.op("scalar", lambda e, par=par: e.copy(
                p.mu_sb[0:1, :], p.ps_d[0:1, par, :]), "s_lm", 1)
            self.wait("tensor", "s_lm")
            self.op("tensor", lambda e, par=par: e.matmul(
                p.ps_c[:, par, :], p.ones_sb[0:1, :], p.mu_sb[0:1, :],
                start=True, stop=True), "t_lb", 1)
            self.wait("vector", "t_lb")
            for k in range(KD):
                self.op("vector", lambda e, k=k, cs=cs, par=par: e.tensor_tensor(
                    p.stream[:, k, cs], p.stream[:, k, cs], p.ps_c[:, par, :],
                    ALU.subtract), "v_ls" if k == KD - 1 else None, 1)
            self.wait("scalar", "v_ls")
            for k in range(KD):
                self.op("scalar", lambda e, k=k, cs=cs: e.activation(
                    p.lnbf2[:, k, :], p.stream[:, k, cs], AF.Square), "s_lq" if k == KD - 1 else None, 1)
            self.wait("tensor", "s_lq")
            for k in range(KD):
                self.op("tensor", lambda e, k=k, par=par: e.matmul(
                    p.ps_d[0:1, par, :], p.on1024_sb[:, 0:1], p.lnbf2[:, k, :],
                    start=(k == 0), stop=(k == KD - 1)), "t_lv" if k == KD - 1 else None, 1)
            self.wait("vector", "t_lv")
            self.op("vector", lambda e, par=par: e.tensor_scalar(
                p.std_sb[0:1, :], p.ps_d[0:1, par, :], EPS, None, ALU.add), "v_lv", 1)
            self.wait("scalar", "v_lv")
            self.op("scalar", lambda e: e.activation(
                p.std_sb[0:1, :], p.std_sb[0:1, :], AF.Sqrt), "s_lv", 1)
            self.wait("vector", "s_lv")
            self.op("vector", lambda e: e.reciprocal(p.istd_sb[0:1, :], p.std_sb[0:1, :]), "v_li", 1)
            self.wait("tensor", "v_li")
            self.op("tensor", lambda e, par=par: e.matmul(
                p.ps_c[:, par, :], p.ones_sb[0:1, :], p.istd_sb[0:1, :],
                start=True, stop=True), "t_li", 1)
            self.wait("vector", "t_li")
            for k in range(KD):
                self.op("vector", lambda e, k=k, cs=cs, par=par: e.tensor_tensor(
                    p.stream[:, k, cs], p.stream[:, k, cs], p.ps_c[:, par, :],
                    ALU.mult), "v_ln" if k == KD - 1 else None, 1)

    def p4_l0ffn(self, sc):
        p = self
        self.ff_hst = self.sb("ff_hst", [128, 1, KD, 512], BF16, sc)
        self.ff_w1 = self.sb("ff_w1", [128, KD, 2048], BF16, sc)
        self.ff_w2 = self.sb("ff_w2", [128, 16, 1024], BF16, sc)
        self.ff_mid = self.sb("ff_mid", [128, 16, 512], BF16, sc)
        self.ff_qint = self.sb("ff_qint", [128, 2, 2048], I32, sc)
        self.ff_db = self.sb("ff_db", [128, KD, 512], BF16, sc)
        self.ff_g = self.sb("ff_g", [128, 8], F32, sc)
        self.ff_b = self.sb("ff_b", [128, 8], F32, sc)
        self.ff_b1 = self.sb("ff_b1", [128, 32], F32, sc)
        self.ff_b2 = self.sb("ff_b2", [128, 8], F32, sc)
        self.ff_b2h = self.sb("ff_b2h", [128, 8], F32, sc)
        for dst_, src_ in ((p.ff_g, p.ffg), (p.ff_b, p.ffb), (p.ff_b1, p.fc1b), (p.ff_b2, p.fc2b)):
            self.op("sync", lambda e, d=dst_, s=src_: e.dma_start(d[:, :], s), "f_p", 16)
        self.wait("vector", "f_p")
        self.op("vector", lambda e: e.tensor_scalar(
            p.ff_b2h[:, :], p.ff_b2[:, :], 0.5, None, ALU.mult), "f_bh", 1)
        for c in range(4):
            cs = slice(c * 512, (c + 1) * 512)
            par = c % 2
            for k in range(KD):
                self.op("vector", lambda e, k=k, cs=cs: e.tensor_copy(
                    p.lnbf[:, k, :], p.stream[:, k, cs]), "fl_c" if k == KD - 1 else None, 1)
            self.wait("tensor", "fl_c")
            if c >= 2:
                self.wait("tensor", "fl_d", self.gc("fl_d"))
            for k in range(KD):
                self.op("tensor", lambda e, k=k, par=par: e.matmul(
                    p.ps_d[0:1, par, :], p.on1024_sb[:, 0:1], p.lnbf[:, k, :],
                    start=(k == 0), stop=(k == KD - 1)), "fl_m" if k == KD - 1 else None, 1)
            self.wait("scalar", "fl_m")
            self.op("scalar", lambda e, par=par: e.copy(
                p.mu_sb[0:1, :], p.ps_d[0:1, par, :]), "fl_s", 1)
            self.wait("tensor", "fl_s")
            self.op("tensor", lambda e, par=par: e.matmul(
                p.ps_c[:, par, :], p.ones_sb[0:1, :], p.mu_sb[0:1, :],
                start=True, stop=True), "fl_b", 1)
            self.wait("vector", "fl_b")
            for k in range(KD):
                self.op("vector", lambda e, k=k, cs=cs, par=par: e.tensor_tensor(
                    p.ff_db[:, k, :], p.stream[:, k, cs], p.ps_c[:, par, :],
                    ALU.subtract), "fl_d" if k == KD - 1 else None, 1)
            self.wait("scalar", "fl_d")
            for k in range(KD):
                self.op("scalar", lambda e, k=k: e.activation(
                    p.lnbf[:, k, :], p.ff_db[:, k, :], AF.Square), "fl_q" if k == KD - 1 else None, 1)
            self.wait("tensor", "fl_q")
            for k in range(KD):
                self.op("tensor", lambda e, k=k, par=par: e.matmul(
                    p.ps_d[0:1, par, :], p.on1024_sb[:, 0:1], p.lnbf[:, k, :],
                    start=(k == 0), stop=(k == KD - 1)), "fl_v" if k == KD - 1 else None, 1)
            self.wait("vector", "fl_v")
            self.op("vector", lambda e, par=par: e.tensor_scalar(
                p.std_sb[0:1, :], p.ps_d[0:1, par, :], EPS, None, ALU.add), "fl_e", 1)
            self.wait("scalar", "fl_e")
            self.op("scalar", lambda e: e.activation(
                p.std_sb[0:1, :], p.std_sb[0:1, :], AF.Sqrt), "fl_sq", 1)
            self.wait("vector", "fl_sq")
            self.op("vector", lambda e: e.reciprocal(p.istd_sb[0:1, :], p.std_sb[0:1, :]), "fl_i", 1)
            self.wait("tensor", "fl_i")
            self.op("tensor", lambda e, par=par: e.matmul(
                p.ps_c[:, par, :], p.ones_sb[0:1, :], p.istd_sb[0:1, :],
                start=True, stop=True), "fl_ib", 1)
            self.wait("vector", "fl_ib")
            self.wait("vector", "y_fh0", None)
            for k in range(KD):
                self.op("vector", lambda e, k=k, par=par: e.tensor_tensor(
                    p.ff_db[:, k, :], p.ff_db[:, k, :], p.ps_c[:, par, :],
                    ALU.mult), None)
                self.op("vector", lambda e, k=k, par=par: e.tensor_scalar(
                    p.ff_hst[:, 0, k, :], p.ff_db[:, k, :], p.ff_g[:, k:k + 1],
                    p.ff_b[:, k:k + 1], ALU.mult, ALU.add), "fl_h" if k == KD - 1 else None, 1)
            self.wait("sync", "fl_h")
            self.op("sync", lambda e, c=c, par=par: e.dma_start(
                p.h_scr.rearrange("(a q) f -> q a f", q=128)[:, :, c * 512:(c + 1) * 512],
                p.ff_hst[:, 0, :, :]), "y_fh0", 16)
        self.qints = 0
        for fh in range(2):
            for g in range(KD):
                qpar = self.qints % 2
                self.qints += 1
                sq = "y_fq" if qpar == 0 else "y_fq1"
                self.wait("sync", "v_dq", max(0, self.gc("v_dq") - 4))
                self.op("sync", lambda e, g=g, fh=fh, qpar=qpar: e.dma_start(
                    p.ff_qint[:, qpar, :], p.fc1q[g * 128:(g + 1) * 128,
                                                  fh * 2048:(fh + 1) * 2048]), sq, 16)
                self.wait("sync", "v_dq", self.gc("v_dq"))
                self.op("sync", lambda e, g=g, fh=fh: e.dma_start(
                    p.row_sb[0:1, :], p.fc1s[g:g + 1, fh * 2048:(fh + 1) * 2048]), "y_fs", 16)
                self.wait("tensor", sq)
                self.wait("tensor", "y_fs")
                if fh == 1 and g == 0:
                    self.wait("tensor", "t_f1", self.gc("t_f1"))
                for n in range(4):
                    bp = n % 2
                    if n >= 2 or g > 0 or fh > 0:
                        self.wait("tensor", "v_dq", self.gc("v_dq") - 1)
                    self.op("tensor", lambda e, n=n, bp=bp: e.matmul(
                        p.ps_c[:, bp, :], p.ones_sb[0:1, :], p.row_sb[0:1, n * 512:(n + 1) * 512],
                        start=True, stop=True), "t_db", 1)
                    self.wait("vector", "t_db")
                    self.op("vector", lambda e, g=g, n=n, bp=bp, qpar=qpar: e.scalar_tensor_tensor(
                        p.ff_w1[:, g, n * 512:(n + 1) * 512],
                        p.ff_qint[:, qpar, n * 512:(n + 1) * 512], -128.0,
                        p.ps_c[:, bp, :], ALU.add, ALU.mult), "v_dq", 1)
            for g2 in range(16):
                qpar = self.qints % 2
                self.qints += 1
                sq = "y_fq" if qpar == 0 else "y_fq1"
                self.wait("sync", "v_dq", max(0, self.gc("v_dq") - 2))
                self.op("sync", lambda e, g2=g2, fh=fh, qpar=qpar: e.dma_start(
                    p.ff_qint[:, qpar, 0:1024], p.fc2q[fh * 2048 + g2 * 128:fh * 2048 + (g2 + 1) * 128, :]),
                    sq, 16)
                self.wait("sync", "v_dq", self.gc("v_dq"))
                self.op("sync", lambda e, g2=g2, fh=fh: e.dma_start(
                    p.row_sb[0:1, 0:1024], p.fc2s[fh * 16 + g2:fh * 16 + g2 + 1, :]), "y_fs", 16)
                self.wait("tensor", sq)
                self.wait("tensor", "y_fs")
                if fh == 1 and g2 == 0:
                    self.wait("tensor", "t_f2", self.gc("t_f2"))
                for n in range(2):
                    bp = n % 2
                    self.wait("tensor", "v_dq", self.gc("v_dq") - 1)
                    self.op("tensor", lambda e, n=n, bp=bp: e.matmul(
                        p.ps_c[:, bp, :], p.ones_sb[0:1, :], p.row_sb[0:1, n * 512:(n + 1) * 512],
                        start=True, stop=True), "t_db", 1)
                    self.wait("vector", "t_db")
                    self.op("vector", lambda e, g2=g2, n=n, bp=bp, qpar=qpar: e.scalar_tensor_tensor(
                        p.lnbf[:, 0:4, :].rearrange("q a f -> q (a f)")[:, n * 512:(n + 1) * 512],
                        p.ff_qint[:, qpar, n * 512:(n + 1) * 512], -128.0,
                        p.ps_c[:, bp, :], ALU.add, ALU.mult), None)
                    self.op("vector", lambda e, g2=g2, n=n: e.tensor_scalar(
                        p.ff_w2[:, g2, n * 512:(n + 1) * 512],
                        p.lnbf[:, 0:4, :].rearrange("q a f -> q (a f)")[:, n * 512:(n + 1) * 512],
                        0.5, None, ALU.mult), "v_dq", 1)
            for c in range(4):
                hpar = 0
                self.wait("sync", "y_fh0")
                if fh > 0 or c >= 2:
                    self.wait("sync", "t_f1", self.gc("t_f1"))
                self.wait("sync", "t_f1", self.gc("t_f1"))
                self.op("sync", lambda e, c=c: e.dma_start(
                    p.ff_hst[:, 0, :, :],
                    p.h_scr.rearrange("(a q) f -> q a f", q=128)[:, :, c * 512:(c + 1) * 512]),
                    "y_fc", 16)
                self.wait("tensor", "y_fc")
                self.wait("tensor", "v_dq")
                for ft in range(16):
                    g = self.gc("t_f1")
                    gp = g % 2
                    if g >= 2:
                        self.wait("tensor", "s_f1", g - 1)
                    for k in range(KD):
                        self.op("tensor", lambda e, ft=ft, k=k, gp=gp: e.matmul(
                            p.ps_a[:, gp, :], p.ff_w1[:, k, ft * 128:(ft + 1) * 128],
                            p.ff_hst[:, 0, k, :], start=(k == 0), stop=(k == KD - 1)),
                            "t_f1" if k == KD - 1 else None, 1)
                    self.wait("scalar", "t_f1")
                    if ft == 0:
                        self.wait("scalar", "t_f2", self.gc("t_f2"))
                    self.op("scalar", lambda e, ft=ft, fh=fh, gp=gp: e.activation(
                        p.ff_mid[:, ft, :], p.ps_a[:, gp, :], AF.Silu,
                        bias=p.ff_b1[:, fh * 16 + ft:fh * 16 + ft + 1]), "s_f1", 1)
                for dt_ in range(8):
                    g = self.gc("t_f2")
                    gp = g % 2
                    if g >= 2:
                        self.wait("tensor", "v_f2", g - 1)
                    if dt_ == 0:
                        self.wait("tensor", "s_f1", self.gc("s_f1"))
                    for kf in range(16):
                        self.op("tensor", lambda e, dt_=dt_, kf=kf, gp=gp: e.matmul(
                            p.ps_b[:, gp, :], p.ff_w2[:, kf, dt_ * 128:(dt_ + 1) * 128],
                            p.ff_mid[:, kf, :], start=(kf == 0), stop=(kf == 15)),
                            "t_f2" if kf == 15 else None, 1)
                    self.wait("vector", "t_f2")
                    if fh == 0:
                        self.op("vector", lambda e, dt_=dt_, c=c, gp=gp: e.tensor_tensor(
                            p.stream[:, dt_, c * 512:(c + 1) * 512],
                            p.stream[:, dt_, c * 512:(c + 1) * 512], p.ps_b[:, gp, :],
                            ALU.add), "v_f2", 1)
                    else:
                        self.op("vector", lambda e, dt_=dt_, c=c, gp=gp: e.scalar_tensor_tensor(
                            p.stream[:, dt_, c * 512:(c + 1) * 512],
                            p.ps_b[:, gp, :], p.ff_b2h[:, dt_:dt_ + 1],
                            p.stream[:, dt_, c * 512:(c + 1) * 512], ALU.add, ALU.add), "v_f2", 1)


    def p5_l1attn(self, sc):
        p = self
        self.a5_w4 = self.sb("a5_w4", [128, KD, 4096], BF16, sc)
        self.a5_bb = self.sb("a5_bb", [128, 3, 1024], BF16, sc)
        self.a5_bo = self.sb("a5_bo", [128, 8], F32, sc)
        self.a5_xc = self.lnbf
        self.a5_qkv = self.sb("a5_qkv", [128, 3, 4, 1024], BF16, sc)
        self.a5_prod = self.sb("a5_prod", [128, 1, 1024], BF16, sc)
        self.a5_S = self.sb("a5_S", [128, 256], F32, sc)
        self.a5_P = self.sb("a5_P", [128, 256], F32, sc)
        self.a5_nm = self.sb("a5_nm", [128, 16], F32, sc)
        self.a5_dn = self.sb("a5_dn", [128, 16], F32, sc)
        self.a5_rc = self.sb("a5_rc", [128, 16], F32, sc)
        self.a5_at = self.sb("a5_at", [128, 4, 1024], BF16, sc)
        self.a5_aT = self.sb("a5_aT", [128, KD, 512], BF16, sc)
        for i, w in enumerate((p.wq, p.wk, p.wv, p.wo)):
            self.op("gpsimd", lambda e, i=i, w=w: e.dma_start(
                p.a5_w4[:, :, i * 1024:(i + 1) * 1024],
                w.rearrange("(a q) f -> q a f", q=128)), "g5_w", 16)
        self.op("sync", lambda e: e.dma_start(p.a5_bo[:, :], p.bo), "y5_b", 16)
        for i, b in enumerate((p.bq, p.bk, p.bv)):
            self.wait("sync", "v5_bb", self.gc("v5_bb"))
            self.op("sync", lambda e, b=b: e.dma_start(p.row_sb[0:1, 0:1024], b), "y5_b", 16)
            self.wait("tensor", "y5_b")
            for n in range(2):
                bp = n % 2
                self.wait("tensor", "v5_bb", self.gc("v5_bb"))
                self.op("tensor", lambda e, n=n, bp=bp: e.matmul(
                    p.ps_c[:, bp, :], p.ones_sb[0:1, :], p.row_sb[0:1, n * 512:(n + 1) * 512],
                    start=True, stop=True), "t5_bb", 1)
                self.wait("vector", "t5_bb")
                self.op("vector", lambda e, i=i, n=n, bp=bp: e.tensor_copy(
                    p.a5_bb[:, i, n * 512:(n + 1) * 512], p.ps_c[:, bp, :]), "v5_bb", 1)
        self.wait("tensor", "g5_w")
        self.wait("vector", "v5_bb")
        for c in range(4):
            cs = slice(c * 512, (c + 1) * 512)
            self.wait("vector", "t5_p", self.gc("t5_p"))
            for k in range(KD):
                self.op("vector", lambda e, k=k, cs=cs: e.tensor_copy(
                    p.a5_xc[:, k, :], p.stream[:, k, cs]), "v5_x" if k == KD - 1 else None, 1)
            self.wait("tensor", "v5_x")
            for i in range(3):
                for tt in range(4):
                    for n in range(2):
                        g = self.gc("t5_p")
                        gp = g % 2
                        if g >= 2:
                            self.wait("tensor", "v5_p", g - 1)
                        for k in range(KD):
                            self.op("tensor", lambda e, i=i, tt=tt, n=n, k=k, gp=gp: e.matmul(
                                p.ps_a[:, gp, :], p.a5_xc[:, k, tt * 128:(tt + 1) * 128],
                                p.a5_w4[:, k, i * 1024 + n * 512:i * 1024 + (n + 1) * 512],
                                start=(k == 0), stop=(k == KD - 1)), "t5_p" if k == KD - 1 else None, 1)
                        self.wait("vector", "t5_p")
                        self.op("vector", lambda e, i=i, tt=tt, n=n, gp=gp: e.tensor_tensor(
                            p.a5_qkv[:, i, tt, n * 512:(n + 1) * 512], p.ps_a[:, gp, :],
                            p.a5_bb[:, i, n * 512:(n + 1) * 512], ALU.add), "v5_p", 1)
            self.wait("vector", "v5_p")
            self.wait("vector", "t5_t", self.gc("t5_t"))
            for tt in range(4):
                qv = p.a5_qkv
                for h in range(16):
                    pp = 0
                    self.op("vector", lambda e, tt=tt, h=h, pp=pp, qv=qv: e.tensor_tensor(
                        p.a5_prod[:, pp, :].rearrange("q (a b) -> q a b", a=16),
                        qv[:, 0, tt, h * 64:(h + 1) * 64].unsqueeze(1).broadcast_to([128, 16, 64]),
                        qv[:, 1, tt, :].rearrange("q (a b) -> q a b", a=16),
                        ALU.mult), None)
                    self.op("vector", lambda e, h=h, pp=pp: e.tensor_reduce(
                        p.a5_S[:, h * 16:(h + 1) * 16],
                        p.a5_prod[:, pp, :].rearrange("q (a b) -> q a b", a=16),
                        AX.X, ALU.add), None)
                self.op("vector", lambda e: e.tensor_reduce(
                    p.a5_nm[:, :], p.a5_S[:, :].rearrange("q (a b) -> q a b", a=16),
                    AX.X, ALU.max, negate=True), None)
                for h in range(16):
                    self.op("vector", lambda e, h=h: e.tensor_scalar(
                        p.a5_S[:, h * 16:(h + 1) * 16], p.a5_S[:, h * 16:(h + 1) * 16],
                        p.a5_nm[:, h:h + 1], None, ALU.add),
                        "v5_s" if h == 15 else None, 1)
                self.wait("scalar", "v5_s")
                self.op("scalar", lambda e: e.activation(
                    p.a5_P[:, :], p.a5_S[:, :], AF.Exp, scale=0.125), "s5_x", 1)
                self.wait("vector", "s5_x")
                self.op("vector", lambda e: e.tensor_reduce(
                    p.a5_dn[:, :], p.a5_P[:, :].rearrange("q (a b) -> q a b", a=16),
                    AX.X, ALU.add), "v5_dn", 1)
                self.wait("scalar", "v5_dn")
                self.op("scalar", lambda e: e.activation(
                    p.a5_rc[:, :], p.a5_dn[:, :], AF.Ln), "s5_l", 1)
                self.wait("vector", "s5_l")
                self.op("vector", lambda e: e.tensor_scalar(
                    p.a5_rc[:, :], p.a5_rc[:, :], -1.0, None, ALU.mult), "v5_ng", 1)
                self.wait("scalar", "v5_ng")
                self.op("scalar", lambda e: e.activation(
                    p.a5_rc[:, :], p.a5_rc[:, :], AF.Exp), "s5_e", 1)
                self.wait("vector", "s5_e")
                for h in range(16):
                    self.op("vector", lambda e, h=h: e.tensor_scalar(
                        p.a5_P[:, h * 16:(h + 1) * 16], p.a5_P[:, h * 16:(h + 1) * 16],
                        p.a5_rc[:, h:h + 1], None, ALU.mult), None)
                for h in range(16):
                    self.op("vector", lambda e, tt=tt, h=h, qv=qv: e.tensor_scalar(
                        p.a5_at[:, tt, h * 64:(h + 1) * 64], qv[:, 2, tt, 0:64],
                        p.a5_P[:, h * 16:h * 16 + 1], None, ALU.mult), None)
                    for e8 in range(1, 16):
                        self.op("vector", lambda e, tt=tt, h=h, e8=e8, qv=qv: e.scalar_tensor_tensor(
                            p.a5_at[:, tt, h * 64:(h + 1) * 64],
                            qv[:, 2, tt, e8 * 64:(e8 + 1) * 64],
                            p.a5_P[:, h * 16 + e8:h * 16 + e8 + 1],
                            p.a5_at[:, tt, h * 64:(h + 1) * 64], ALU.mult, ALU.add),
                            "v5_pv" if (h == 15 and e8 == 15) else None, 1)
            self.wait("tensor", "v5_pv")
            for tt in range(4):
                for dtl in range(8):
                    g = self.gc("t5_t")
                    bp = g % 2
                    if g >= 2:
                        self.wait("tensor", "s5_t", g - 1)
                    def _tp(e, tt=tt, dtl=dtl, bp=bp):
                        with p.nc.allow_low_precision(reason="bf16 transpose passthrough"):
                            return e.transpose(
                                p.ps_c[:, bp, 0:64].bitcast(BF16),
                                p.a5_at[:, tt, dtl * 128:(dtl + 1) * 128], p.idnbf_sb[:, :])
                    self.op("tensor", _tp, "t5_t", 1)
                    self.wait("scalar", "t5_t")
                    if g == 0 and c > 0:
                        self.wait("scalar", "t5_o", self.gc("t5_o"))
                    self.op("scalar", lambda e, tt=tt, dtl=dtl, bp=bp: e.activation(
                        p.a5_aT[:, dtl, tt * 128:(tt + 1) * 128], p.ps_c[:, bp, 0:64].bitcast(BF16),
                        AF.Copy), "s5_t", 1)
            self.wait("tensor", "s5_t")
            for dtl in range(8):
                g = self.gc("t5_o")
                gp = g % 2
                if g >= 2:
                    self.wait("tensor", "v5_o", g - 1)
                for k in range(KD):
                    self.op("tensor", lambda e, dtl=dtl, k=k, gp=gp: e.matmul(
                        p.ps_b[:, gp, :], p.a5_w4[:, k, 3 * 1024 + dtl * 128:3 * 1024 + (dtl + 1) * 128],
                        p.a5_aT[:, k, :], start=(k == 0), stop=(k == KD - 1)),
                        "t5_o" if k == KD - 1 else None, 1)
                self.wait("vector", "t5_o")
                self.op("vector", lambda e, dtl=dtl, cs=cs, gp=gp: e.scalar_tensor_tensor(
                    p.stream[:, dtl, cs], p.ps_b[:, gp, :], p.a5_bo[:, dtl:dtl + 1],
                    p.stream[:, dtl, cs], ALU.add, ALU.add), "v5_o", 1)


    def p6_l1ffn(self, sc):
        p = self
        self.h6st = self.sb("h6st", [128, 1, KD, 512], BF16, sc)
        self.p6s8 = self.sb("p6s8", [128, 2048], F32, sc)
        self.w1T6 = self.sb("w1T6", [128, KD, 2048], BF16, sc)
        self.w2T6 = self.sb("w2T6", [128, 16, 1024], BF16, sc)
        self.qo6 = self.sb("qo6", [128, 2048], BF16, sc)
        self.mid6 = self.sb("mid6", [128, 16, 512], BF16, sc)
        self.gb6 = self.sb("gb6", [128, 512], BF16, sc)
        self.gateT = self.sb("gateT", [8, 2048], F32, sc)
        self.rwT6 = self.sb("rwT6", [128, KD, 8], F32, sc)
        self.rg6 = self.sb("rg6", [128, 8], F32, sc)
        self.rb6 = self.sb("rb6", [128, 8], F32, sc)
        self.rs6 = self.sb("rs6", [8, 256], F32, sc)
        self.tk6 = self.sb("tk6", [128, 16], F32, sc)
        self.qsc = self.sb("qsc", [128, 4, 8], F32, sc)
        self.sm6 = self.sb("sm6", [128, 8, 8], F32, sc)
        self.g86 = self.sb("g86", [128, 8], F32, sc)
        df32 = self.p6s8[:, :].rearrange("q (a f) -> q a f", a=KD)
        hf32 = df32
        qw = self.p6s8
        self.op("sync", lambda e: e.dma_start(
            p.rwT6[:, :, :], p.rwT.rearrange("(a q) f -> q a f", q=128)), "y6_p", 16)
        self.op("sync", lambda e: e.dma_start(p.rg6[:, :], p.g1), "y6_p", 16)
        self.op("sync", lambda e: e.dma_start(p.rb6[:, :], p.b1), "y6_p", 16)
        self.wait("vector", "y6_p")
        self.wait("tensor", "y6_p")
        self.op("vector", lambda e: e.memset(p.sm6[:, :, :], 0.0), None)
        self.op("vector", lambda e: e.memset(p.qsc[:, :, :], 0.0), "vsp", 1)
        self.wait("vector", "vsp")
        for c8 in range(8):
            cs = slice(c8 * 256, (c8 + 1) * 256)
            par = c8 % 2
            hpar = 0
            for k in range(KD):
                self.op("vector", lambda e, k=k, cs=cs: e.tensor_copy(
                    p.lnbf[:, k, 0:256], p.stream[:, k, cs]), "l6_c" if k == KD - 1 else None, 1)
            self.wait("tensor", "l6_c")
            if c8 >= 2:
                self.wait("tensor", "l6_d", self.gc("l6_d"))
            for k in range(KD):
                self.op("tensor", lambda e, k=k, par=par: e.matmul(
                    p.ps_d[0:1, par, 0:256], p.on1024_sb[:, 0:1], p.lnbf[:, k, 0:256],
                    start=(k == 0), stop=(k == KD - 1)), "l6_m" if k == KD - 1 else None, 1)
            self.wait("scalar", "l6_m")
            self.op("scalar", lambda e, par=par: e.copy(
                p.mu_sb[0:1, 0:256], p.ps_d[0:1, par, 0:256]), "l6_s", 1)
            self.wait("tensor", "l6_s")
            self.op("tensor", lambda e, par=par: e.matmul(
                p.ps_c[:, par, 0:256], p.ones_sb[0:1, :], p.mu_sb[0:1, 0:256],
                start=True, stop=True), "l6_b", 1)
            self.wait("vector", "l6_b")
            self.wait("vector", "t6_r", self.gc("t6_r"))
            for k in range(KD):
                self.op("vector", lambda e, k=k, cs=cs, par=par, df32=df32: e.tensor_tensor(
                    df32[:, k, :], p.stream[:, k, cs], p.ps_c[:, par, 0:256],
                    ALU.subtract), "l6_d" if k == KD - 1 else None, 1)
            self.wait("scalar", "l6_d")
            for k in range(KD):
                self.op("scalar", lambda e, k=k, df32=df32: e.activation(
                    p.lnbf[:, k, 0:256], df32[:, k, :], AF.Square), "l6_q" if k == KD - 1 else None, 1)
            self.wait("tensor", "l6_q")
            for k in range(KD):
                self.op("tensor", lambda e, k=k, par=par: e.matmul(
                    p.ps_d[0:1, par, 0:256], p.on1024_sb[:, 0:1], p.lnbf[:, k, 0:256],
                    start=(k == 0), stop=(k == KD - 1)), "l6_v" if k == KD - 1 else None, 1)
            self.wait("vector", "l6_v")
            self.op("vector", lambda e, par=par: e.tensor_scalar(
                p.std_sb[0:1, 0:256], p.ps_d[0:1, par, 0:256], EPS, None, ALU.add), "l6_e", 1)
            self.wait("scalar", "l6_e")
            self.op("scalar", lambda e: e.activation(
                p.std_sb[0:1, 0:256], p.std_sb[0:1, 0:256], AF.Sqrt), "l6_sq", 1)
            self.wait("vector", "l6_sq")
            self.op("vector", lambda e: e.reciprocal(p.istd_sb[0:1, 0:256], p.std_sb[0:1, 0:256]), "l6_i", 1)
            self.wait("tensor", "l6_i")
            self.op("tensor", lambda e, par=par: e.matmul(
                p.ps_c[:, par, 0:256], p.ones_sb[0:1, :], p.istd_sb[0:1, 0:256],
                start=True, stop=True), "l6_ib", 1)
            self.wait("vector", "l6_ib")
            if c8 >= 2:
                self.wait("vector", "y6_h0", None)
            for k in range(KD):
                self.op("vector", lambda e, k=k, par=par, df32=df32: e.tensor_tensor(
                    df32[:, k, :], df32[:, k, :], p.ps_c[:, par, 0:256],
                    ALU.mult), None)
                self.op("vector", lambda e, k=k, c8=c8, hpar=hpar, df32=df32: e.tensor_scalar(
                    p.h6st[:, hpar, k, (c8 % 2) * 256:(c8 % 2) * 256 + 256],
                    df32[:, k, :], p.rg6[:, k:k + 1],
                    p.rb6[:, k:k + 1], ALU.mult, ALU.add), None)
                self.op("vector", lambda e, k=k, df32=df32, hf32=hf32: e.tensor_scalar(
                    hf32[:, k, 0:256], df32[:, k, :], p.rg6[:, k:k + 1],
                    p.rb6[:, k:k + 1], ALU.mult, ALU.add), "l6_h" if k == KD - 1 else None, 1)
            if c8 % 2 == 1:
                self.wait("sync", "l6_h")
                self.op("sync", lambda e, c8=c8: e.dma_start(
                    p.h_scr.rearrange("(a q) f -> q a f", q=128)[:, :, (c8 // 2) * 512:(c8 // 2 + 1) * 512],
                    p.h6st[:, 0, :, :]), "y6_h0", 16)
            self.wait("tensor", "l6_h")
            if c8 >= 2:
                self.wait("tensor", "s6_rs", self.gc("s6_rs"))
            for k in range(KD):
                self.op("tensor", lambda e, k=k, par=par, hf32=hf32: e.matmul(
                    p.ps_d[0:8, par, 0:256], p.rwT6[:, k, :], hf32[:, k, 0:256],
                    start=(k == 0), stop=(k == KD - 1)), "t6_r" if k == KD - 1 else None, 1)
            self.wait("scalar", "t6_r")
            self.op("scalar", lambda e, par=par: e.copy(
                p.rs6[0:8, :], p.ps_d[0:8, par, 0:256]), "s6_rs", 1)
            for t2 in range(2):
                gtile = c8 * 2 + t2
                bp = gtile % 2
                self.wait("tensor", "s6_rs")
                if gtile >= 2:
                    self.wait("tensor", "v6_st", self.gc("v6_st"))
                self.op("tensor", lambda e, t2=t2, bp=bp: e.transpose(
                    p.ps_b[:, bp, 0:8], p.rs6[0:8, t2 * 128:(t2 + 1) * 128],
                    p.idnf_sb[0:8, 0:8]), "t6_st", 1)
                self.wait("vector", "t6_st")
                self.op("vector", lambda e, bp=bp: e.tensor_copy(
                    p.tk6[:, 0:8], p.ps_b[:, bp, 0:8]), "v6_st", 1)
                self.op("vector", lambda e: e.tensor_reduce(
                    p.sm6[:, 0, 0:1], p.tk6[:, 0:8], AX.X, ALU.max, negate=True),
                    "vsp", 1)
                self.wait("vector", "vsp")
                self.op("vector", lambda e: e.tensor_scalar(
                    p.tk6[:, 0:8], p.tk6[:, 0:8], p.sm6[:, 0, 0:1], None,
                    ALU.add), "v6_sm", 1)
                self.wait("scalar", "v6_sm")
                self.op("scalar", lambda e: e.activation(
                    p.tk6[:, 8:16], p.tk6[:, 0:8], AF.Exp, accum_out=p.sm6[:, 1, 0:1]), "s6_x", 1)
                self.wait("scalar", "s6_x")
                self.op("scalar", lambda e: e.activation(
                    p.sm6[:, 2, 0:1], p.sm6[:, 1, 0:1], AF.Ln), "s6_l", 1)
                self.wait("vector", "s6_l")
                self.op("vector", lambda e: e.tensor_scalar(
                    p.sm6[:, 3, 0:8], p.sm6[:, 2, 0:8], -1.0, None, ALU.mult), "v6_ng", 1)
                self.wait("scalar", "v6_ng")
                self.op("scalar", lambda e: e.activation(
                    p.sm6[:, 4, 0:1], p.sm6[:, 3, 0:1], AF.Exp), "s6_e", 1)
                self.wait("vector", "s6_e")
                self.op("vector", lambda e: e.tensor_scalar(
                    p.tk6[:, 8:16], p.tk6[:, 8:16], p.sm6[:, 4, 0:1], None,
                    ALU.mult), "vsp", 1)
                self.wait("vector", "vsp")
                self.op("vector", lambda e: e.tensor_reduce(
                    p.sm6[:, 5, 0:1], p.tk6[:, 8:16], AX.X, ALU.max), "vsp", 1)
                self.wait("vector", "vsp")
                self.op("vector", lambda e: e.tensor_scalar(
                    p.g86[:, 0:8], p.tk6[:, 8:16], p.sm6[:, 5, 0:1], None,
                    ALU.is_ge), "vsp", 1)
                self.wait("vector", "vsp")
                self.op("vector", lambda e: e.tensor_tensor(
                    p.g86[:, 0:8], p.g86[:, 0:8], p.pw2_sb[:, 0:8], ALU.mult), "vsp", 1)
                self.wait("vector", "vsp")
                self.op("vector", lambda e: e.tensor_reduce(
                    p.sm6[:, 6, 0:1], p.g86[:, 0:8], AX.X, ALU.max), "vsp", 1)
                self.wait("vector", "vsp")
                self.op("vector", lambda e: e.tensor_scalar(
                    p.g86[:, 0:8], p.g86[:, 0:8], p.sm6[:, 6, 0:1], None,
                    ALU.is_ge), "vsp", 1)
                self.wait("vector", "vsp")
                self.op("vector", lambda e: e.tensor_scalar(
                    p.g86[:, 0:8], p.g86[:, 0:8], p.sm6[:, 5, 0:1], None,
                    ALU.mult), "v6_g", 1)
                self.wait("tensor", "v6_g")
                self.op("tensor", lambda e, bp=bp: e.transpose(
                    p.ps_b[0:8, bp, 128:256], p.g86[:, 0:8], p.idnf_sb[:, :]), "t6_g", 1)
                self.wait("vector", "t6_g")
                self.op("vector", lambda e, gtile=gtile, bp=bp: e.tensor_copy(
                    p.gateT[0:8, gtile * 128:(gtile + 1) * 128],
                    p.ps_b[0:8, bp, 128:256]), "v6_st", 1)
        self.wait("tensor", "v6_st")
        self.wait("tensor", "v6_g")
        self.wait("sync", "s6_rs")
        v6g0 = self.gc("v6_g")
        self.q_last = {0: None, 1: None}
        for ex in range(9):
            epar = ex % 2
            tiles = [(0, t, 1024, (t % 2) * 1024, [t % 2]) for t in range(16)] + \
                    [(1, t, 2048, 0, [0, 1]) for t in range(8)]

            def q_dep(slots):
                dep = {}
                for s in slots:
                    if p.q_last[s]:
                        for kk, vv in p.q_last[s].items():
                            dep[kk] = max(dep.get(kk, 0), vv)
                return dep

            q_inval = {}

            def rec_in(j, ex=ex):
                mat, t_i, NF, off, slots = tiles[j]
                dep = q_dep(slots)
                if dep.get("s6_g"):
                    self.wait("sync", "s6_g", dep["s6_g"])
                srcw = p.ew1 if mat == 0 else p.ew2
                sw = "y6_qw" if j % 2 == 0 else "y6_qw2"
                self.op("sync", lambda e, ex=ex, t_i=t_i, NF=NF, off=off, srcw=srcw, qw=qw: e.dma_start(
                    qw[:, off:off + NF], srcw[ex, t_i * 128:(t_i + 1) * 128, :]), sw, 16)
                q_inval[j] = (sw, self.gc(sw))

            for j in range(len(tiles)):
                if j == 0:
                    rec_in(0)
                    rec_in(1)
                mat, t_i, NF, off, slots = tiles[j]
                par = slots[0] if len(slots) == 1 else 0
                arow, trow, nrow = 7 - par, 2 * par, 2 * par + 1
                dep = q_dep(slots)
                self.wait("scalar", q_inval[j][0], q_inval[j][1])
                if dep.get("y6_qo"):
                    self.wait("scalar", "y6_qo", dep["y6_qo"])
                if dep.get("y6_qo2"):
                    self.wait("scalar", "y6_qo2", dep["y6_qo2"])
                if ex == 0 and j == 1:
                    self.wait("scalar", "v6_g", v6g0)
                self.op("scalar", lambda e, NF=NF, off=off, arow=arow, qw=qw: e.activation(
                    p.qo6[:, off:off + NF], qw[:, off:off + NF], AF.Abs,
                    accum_out=p.sm6[:, arow, 0:1]), "s6_a", 1)
                self.wait("vector", "s6_a")
                self.op("vector", lambda e, NF=NF, arow=arow, trow=trow: e.tensor_scalar(
                    p.qsc[:, trow, 0:8], p.sm6[:, arow, 0:8], 0.5 / NF, None,
                    ALU.mult), "vsp", 1)
                self.wait("vector", "vsp")
                self.op("vector", lambda e, trow=trow: e.tensor_scalar(
                    p.qsc[:, trow, 0:8], p.qsc[:, trow, 0:8], 0.5e-5, None,
                    ALU.max), "vsp", 1)
                self.wait("vector", "vsp")
                self.op("vector", lambda e, trow=trow, nrow=nrow: e.tensor_scalar(
                    p.qsc[:, nrow, 0:8], p.qsc[:, trow, 0:8], -1.0, None,
                    ALU.mult), "v6_t", 1)
                self.wait("scalar", "v6_t")
                if dep.get("v6_q"):
                    self.wait("scalar", "v6_q", dep["v6_q"])
                self.op("scalar", lambda e, NF=NF, off=off, nrow=nrow, qw=qw: e.activation(
                    p.qo6[:, off:off + NF], qw[:, off:off + NF], AF.Sign,
                    bias=p.qsc[:, nrow, 0:1]), None)
                self.op("scalar", lambda e, NF=NF, off=off, trow=trow, qw=qw: e.activation(
                    p.lnbf[:, :, :].rearrange("q a f -> q (a f)")[:, off:off + NF],
                    qw[:, off:off + NF], AF.Sign, bias=p.qsc[:, trow, 0:1]), "s6_g", 1)
                self.wait("vector", "s6_g")
                self.op("vector", lambda e, NF=NF, off=off: e.tensor_tensor(
                    p.qo6[:, off:off + NF], p.qo6[:, off:off + NF],
                    p.lnbf[:, :, :].rearrange("q a f -> q (a f)")[:, off:off + NF],
                    ALU.add), "vsp", 1)
                self.wait("vector", "vsp")
                self.op("vector", lambda e, NF=NF, off=off, trow=trow: e.tensor_scalar(
                    p.qo6[:, off:off + NF], p.qo6[:, off:off + NF], p.qsc[:, trow, 0:1], None,
                    ALU.mult), "v6_q", 1)
                self.wait("sync", "v6_q")
                dscr = p.w1q_scr if mat == 0 else p.w2q_scr
                so = "y6_qo" if j % 2 == 0 else "y6_qo2"
                self.op("sync", lambda e, t_i=t_i, NF=NF, off=off, epar=epar, dscr=dscr: e.dma_start(
                    dscr[epar, t_i * 128:(t_i + 1) * 128, :], p.qo6[:, off:off + NF]), so, 16)
                snap = {"y6_qo": self.gc("y6_qo"), "y6_qo2": self.gc("y6_qo2"),
                        "s6_g": self.gc("s6_g"), "v6_q": self.gc("v6_q")}
                for s in slots:
                    self.q_last[s] = snap
                if j + 2 < len(tiles) and len(tiles[j + 1][4]) == 1 and len(tiles[j + 2][4]) == 1:
                    rec_in(j + 2)
                elif j + 1 < len(tiles) and (j + 2 >= len(tiles) or len(tiles[j + 2][4]) == 2) \
                        and (j + 1) not in q_inval:
                    rec_in(j + 1)
            self.wait("sync", "y6_qo")
            self.wait("sync", "y6_qo2")
            self.wait("sync", "t6_f1", self.gc("t6_f1"))
            self.wait("sync", "t6_f2", self.gc("t6_f2"))
            for dt_ in range(KD):
                self.op("sync", lambda e, dt_=dt_, epar=epar: e.dma_start(
                    p.w1T6[:, dt_, :], p.w1q_scr[epar][:, dt_ * 128:(dt_ + 1) * 128],
                    transpose=True), "y6_wt", 16)
            for ft in range(16):
                self.op("sync", lambda e, ft=ft, epar=epar: e.dma_start(
                    p.w2T6[:, ft, :], p.w2q_scr[epar][:, ft * 128:(ft + 1) * 128],
                    transpose=True), "y6_wt", 16)
            self.wait("tensor", "y6_wt")
            for c in range(4):
                hpar = 0
                self.wait("sync", "y6_h0")
                self.wait("sync", "t6_f1", self.gc("t6_f1"))
                self.op("sync", lambda e, c=c: e.dma_start(
                    p.h6st[:, 0, :, :],
                    p.h_scr.rearrange("(a q) f -> q a f", q=128)[:, :, c * 512:(c + 1) * 512]),
                    "y6_hc", 16)
                self.wait("tensor", "y6_hc")
                for ft in range(16):
                    g = self.gc("t6_f1")
                    gp = g % 2
                    if g >= 2:
                        self.wait("tensor", "s6_f1", g - 1)
                    for k in range(KD):
                        self.op("tensor", lambda e, ft=ft, k=k, gp=gp, hpar=hpar: e.matmul(
                            p.ps_a[:, gp, :], p.w1T6[:, k, ft * 128:(ft + 1) * 128],
                            p.h6st[:, hpar, k, :], start=(k == 0), stop=(k == KD - 1)),
                            "t6_f1" if k == KD - 1 else None, 1)
                    self.wait("scalar", "t6_f1")
                    if ft == 0:
                        self.wait("scalar", "v6_mg", self.gc("v6_mg"))
                        self.wait("scalar", "t6_f2", self.gc("t6_f2"))
                    self.op("scalar", lambda e, ft=ft, gp=gp: e.activation(
                        p.mid6[:, ft, :], p.ps_a[:, gp, :], AF.Silu), "s6_f1", 1)
                if ex < 8:
                    self.wait("tensor", "v6_st")
                    self.wait("tensor", "s6_gb", self.gc("s6_gb"))
                    gbp = c % 2
                    self.op("tensor", lambda e, ex=ex, c=c, gbp=gbp: e.matmul(
                        p.ps_c[:, gbp, :], p.sel8_sb[:, ex * 128:(ex + 1) * 128],
                        p.gateT[0:8, c * 512:(c + 1) * 512], start=True, stop=True), "t6_gb", 1)
                    self.wait("scalar", "t6_gb")
                    self.op("scalar", lambda e, gbp=gbp: e.activation(
                        p.gb6[:, :], p.ps_c[:, gbp, :], AF.Copy), "s6_gb", 1)
                    self.wait("vector", "s6_gb")
                    self.wait("vector", "s6_f1")
                    self.op("vector", lambda e: e.tensor_tensor(
                        p.mid6[:, :, :], p.mid6[:, :, :],
                        p.gb6[:, :].unsqueeze(1).broadcast_to([128, 16, 512]),
                        ALU.mult), "v6_mg", 1)
                    self.wait("tensor", "v6_mg")
                else:
                    self.wait("tensor", "s6_f1")
                for dt_ in range(8):
                    g = self.gc("t6_f2")
                    gp = g % 2
                    if g >= 2:
                        self.wait("tensor", "v6_f2", g - 1)
                    for kf in range(16):
                        self.op("tensor", lambda e, dt_=dt_, kf=kf, gp=gp: e.matmul(
                            p.ps_b[:, gp, :], p.w2T6[:, kf, dt_ * 128:(dt_ + 1) * 128],
                            p.mid6[:, kf, :], start=(kf == 0), stop=(kf == 15)),
                            "t6_f2" if kf == 15 else None, 1)
                    self.wait("vector", "t6_f2")
                    self.op("vector", lambda e, dt_=dt_, c=c, gp=gp: e.tensor_tensor(
                        p.stream[:, dt_, c * 512:(c + 1) * 512],
                        p.stream[:, dt_, c * 512:(c + 1) * 512], p.ps_b[:, gp, :],
                        ALU.add), "v6_f2", 1)


    def p7_store(self):
        p = self
        with contextlib.ExitStack() as sc:
            otm = self.sb("otm", [128, 2, D], F32, sc)
            self.wait("tensor", "v_ln", self.gc("v_ln"))
            for tt in range(16):
                par = tt % 2
                if tt >= 2:
                    self.wait("vector", "y_ot%d" % par, None)
                for dtl in range(8):
                    bp = (tt * 8 + dtl) % 2
                    if tt * 8 + dtl >= 2:
                        self.wait("tensor", "v_ot", self.gc("v_ot") - 1)
                    self.op("tensor", lambda e, tt=tt, dtl=dtl, bp=bp: e.transpose(
                        p.ps_b[:, bp, 0:128], p.stream[:, dtl, tt * 128:(tt + 1) * 128],
                        p.idnf_sb[:, :]), "t_ot", 1)
                    self.wait("vector", "t_ot")
                    self.op("vector", lambda e, tt=tt, dtl=dtl, par=par, bp=bp: e.tensor_copy(
                        otm[:, par, dtl * 128:(dtl + 1) * 128], p.ps_b[:, bp, 0:128]), "v_ot", 1)
                self.wait("sync", "v_ot")
                self.op("sync", lambda e, tt=tt, par=par: e.dma_start(
                    p.out[tt * 128:(tt + 1) * 128, :], otm[:, par, :]), "y_ot%d" % par, 16)
            self.wait("sync", "y_ot0")
            self.wait("sync", "y_ot1")


def build_program(stage=DEF_STAGE):
    nc = bass.Bass("TRN2", target_bir_lowering=False, detect_race_conditions=False)
    mk = MK(nc, stage)
    mk.build()
    return nc


def make_in_maps(inp, ncores=8):
    x = np.ascontiguousarray(inp["L0_attn_x"], dtype=np.float32)
    import ml_dtypes
    c = np.ascontiguousarray
    pm = lambda v, n: c(np.asarray(v, np.float32).reshape(n, 128).T)
    base = {
        "winT0": c(inp["L0_attn_Win0"].T), "bin0": pm(inp["L0_attn_bin0"], 24),
        "woutT0": c(inp["L0_attn_Wout0"].T), "bout0": pm(inp["L0_attn_bout0"], 8),
        "winT1": c(inp["L0_attn_Win1"].T), "bin1": pm(inp["L0_attn_bin1"], 24),
        "woutT1": c(inp["L0_attn_Wout1"].T), "bout1": pm(inp["L0_attn_bout1"], 8),
        "winT2": c(inp["L0_attn_Win2"].T), "bin2": pm(inp["L0_attn_bin2"], 24),
        "woutT2": c(inp["L0_attn_Wout2"].T), "bout2": pm(inp["L0_attn_bout2"], 8),
        "binv0": c(np.asarray(inp["L0_attn_bin0"], np.float32)[None, 2048:]),
        "binv1": c(np.asarray(inp["L0_attn_bin1"], np.float32)[None, 2048:]),
        "binv2": c(np.asarray(inp["L0_attn_bin2"], np.float32)[None, 2048:]),
        "ffg": pm(inp["L0_ffn_ln_gamma"], 8), "ffb": pm(inp["L0_ffn_ln_beta"], 8),
        "fc1q": c(inp["L0_ffn_fc1_q"]), "fc1s": c(inp["L0_ffn_fc1_scales"]),
        "fc1b": pm(inp["L0_ffn_fc1_bias"], 32), "fc2q": c(inp["L0_ffn_fc2_q"]),
        "fc2s": c(inp["L0_ffn_fc2_scales"]), "fc2b": pm(inp["L0_ffn_fc2_bias"], 8),
        "wq": c(inp["L1_attn_Wq"]), "bq": c(np.asarray(inp["L1_attn_bq"], np.float32)[None]),
        "wk": c(inp["L1_attn_Wk"]), "bk": c(np.asarray(inp["L1_attn_bk"], np.float32)[None]),
        "wv": c(inp["L1_attn_Wv"]), "bv": c(np.asarray(inp["L1_attn_bv"], np.float32)[None]),
        "wo": c(inp["L1_attn_Wo"]), "bo": pm(inp["L1_attn_bo"], 8),
        "ew1": c(np.concatenate([inp["L1_ffn_expert_w1"], inp["L1_ffn_shared_w1"][None]], 0)),
        "ew2": c(np.concatenate([inp["L1_ffn_expert_w2"], inp["L1_ffn_shared_w2"][None]], 0)),
        "rwT": c(inp["L1_ffn_router_w"].T), "g1": pm(inp["L1_ffn_ln_gamma"], 8),
        "b1": pm(inp["L1_ffn_ln_beta"], 8),
        "ones128": np.ones((1, 128), np.float32),
        "onescol_bf": np.ones((128, 1), ml_dtypes.bfloat16),
        "on1024_bf": np.full((128, 1), 1.0 / 1024.0, ml_dtypes.bfloat16),
        "idn_bf": np.eye(128, dtype=ml_dtypes.bfloat16),
        "idn_f": np.eye(128, dtype=np.float32),
        "sel8": np.concatenate([np.tile(np.eye(8, dtype=np.float32)[:, h:h + 1], (1, 128))
                                for h in range(8)], axis=1),
        "sel8h": np.concatenate([np.eye(8, dtype=ml_dtypes.bfloat16)[h][None, :].repeat(128, 0) * 0
                                 + np.eye(8, dtype=ml_dtypes.bfloat16)[:, h][None, :].astype(ml_dtypes.bfloat16) * 0
                                 + np.tile((np.arange(8) == h).astype(ml_dtypes.bfloat16)[None, :], (128, 1))
                                 for h in range(8)], axis=1).astype(ml_dtypes.bfloat16),
        "pw2": np.tile((2.0 ** -np.arange(8, dtype=np.float32))[None, :], (128, 1)),
    }
    maps = []
    for core in range(ncores):
        b, c2 = core // 2, core % 2
        m = dict(base)
        m["xwin"] = c(x[b, 448 * c2: 448 * c2 + WIN, :])
        m["xres"] = c(x[b, TC * c2: TC * (c2 + 1), :])
        m["mv"] = np.full((128, 1), float(c2), np.float32)
        m["mv1"] = np.full((128, 1), float(1 - c2), np.float32)
        maps.append(m)
    return maps


def kernel(**inp):
    nc = build_program()
    maps = make_in_maps(inp)
    res = run_bass_kernel_spmd(nc, maps, core_ids=list(range(len(maps))))
    out = np.empty((4, 4096, D), np.float32)
    for core in range(len(maps)):
        b, c2 = core // 2, core % 2
        out[b, TC * c2: TC * (c2 + 1), :] = res.results[core]["out"]
    return out




# version 07 (final)
# speedup vs baseline: 1.0112x; 1.0112x over previous
import os
import contextlib
import numpy as np
import concourse.bass as bass
import concourse.mybir as mybir
from concourse.bass_utils import run_bass_kernel_spmd

F32 = mybir.dt.float32
BF16 = mybir.dt.bfloat16
I32 = mybir.dt.int32
AF = mybir.ActivationFunctionType
ALU = mybir.AluOpType
AX = mybir.AxisListType

D = 1024
KD = 8
TC = 2048
WIN = 3200
ISDH = 0.08838834764831845
EPS = 1e-5
DEF_STAGE = 9
ENGS = ("tensor", "vector", "scalar", "gpsimd", "sync")


class MK:
    def __init__(self, nc, stage):
        self.nc = nc
        self.stage = stage
        self.ctx = contextlib.ExitStack()
        self.prog = {e: [] for e in ENGS}
        self.sems = {}
        self.cnt = {}
        self.nbar = 0

    SEM_ALIAS = {
        "fl_c": "v_lc", "fl_m": "t_lm", "fl_s": "s_lm", "fl_b": "t_lb",
        "fl_d": "v_ls", "fl_q": "s_lq", "fl_v": "t_lv", "fl_e": "v_lv",
        "fl_sq": "s_lv", "fl_i": "v_li", "fl_ib": "t_li", "fl_h": "v_ln",
        "l6_c": "v_lc", "l6_m": "t_lm", "l6_s": "s_lm", "l6_b": "t_lb",
        "l6_d": "v_ls", "l6_q": "s_lq", "l6_v": "t_lv", "l6_e": "v_lv",
        "l6_sq": "s_lv", "l6_i": "v_li", "l6_ib": "t_li", "l6_h": "v_ln",
        "t5_p": "t_pj", "v5_p": "s_pj",
        "t6_f1": "t_f1", "s6_f1": "s_f1", "t6_f2": "t_f2", "v6_f2": "v_f2",
        "t5_o": "t_o", "v5_o": "s_o", "t_o2": "t_o", "v_o2": "s_o",
        "t_ot": "t_xi", "v_ot": "v_xi",
        "t5_bb": "t_g", "v5_bb": "v_g", "t6_gb": "t_g", "s6_gb": "v_g",
        "t_db": "t_g", "t6_r": "t_g", "t6_st": "t_g", "v6_st": "v_g",
        "t6_g": "t_g", "v6_g": "v_g", "t_m3": "t_g", "v_m3": "v_g",
        "v_m2": "v_g", "t_rb": "t_g", "s6_rs": "s_lm", "v6_sm": "v_g",
        "s5_x": "s_x", "s6_x": "s_x", "s6_a": "s_x", "v5_s": "v_g",
        "v5_dn": "v_g", "s5_l": "s_lq", "v5_ng": "v_g", "s5_e": "s_x",
        "v6_sm": "v_g", "s6_l": "s_lq", "v6_ng": "v_g", "s6_e": "s_x",
        "v5_x": "v_g", "v5_pv": "v_sq", "v6_mg": "v_mg", "f_bh": "v_g",
        "f_p": "wl", "y_fq": "wl", "y5_b": "wl", "y6_p": "wl",
        "y6_qw": "wl", "y6_wt": "wl", "c0": "wl", "xbf": "xtt",
        "y_fc": "y_at", "y6_hc": "y_at", "y_in": "y_at", "y_qa": "y_at",
        "y_xr": "y_at", "y_at2": "y_at", "y_l0": "y_at",
        "y_fh0": "y_p0", "y_fh1": "y_p1", "y6_h0": "y_p0", "y6_qo": "y_p1",
        "y_ot0": "y_o0", "y_ot1": "y_o1",
        "g5_w": "wlg",
        "v6_q": "v_dq", "v6_t": "v_g", "s6_g": "s_lq", "y_fs": "xtt",
        "y_fq1": "y_o0", "y6_qw2": "y_o0", "y6_qo2": "y_o1",
    }

    def sem(self, name):
        name = self.SEM_ALIAS.get(name, name)
        if name not in self.sems:
            self.sems[name] = self.ctx.enter_context(self.nc.semaphore(name))
            self.cnt[name] = 0
        return self.sems[name]

    def op(self, eng, build, sem=None, n=1):
        if sem is not None:
            sem = self.SEM_ALIAS.get(sem, sem)
            s = self.sem(sem)
            self.cnt[sem] += n
            self.prog[eng].append(lambda e, b=build, s=s, n=n: b(e).then_inc(s, n))
        else:
            self.prog[eng].append(build)

    def wait(self, eng, sem, val=None):
        sem = self.SEM_ALIAS.get(sem, sem)
        s = self.sem(sem)
        v = self.cnt[sem] if val is None else val
        self.prog[eng].append(lambda e, s=s, v=v: e.wait_ge(s, v))

    def barrier(self, engs=ENGS):
        self.nbar += 1
        s = self.sem("bar")
        self.cnt["bar"] += len(engs)
        tgt = self.cnt["bar"]
        for eng in engs:
            self.prog[eng].append(lambda e, s=s: e.nop().then_inc(s, 1))
            self.prog[eng].append(lambda e, s=s, t=tgt: e.wait_ge(s, t))

    def gc(self, name, d=0):
        name = self.SEM_ALIAS.get(name, name)
        return self.cnt.get(name, d)

    def sb(self, name, shape, dtype, scope=None):
        return (scope or self.ctx).enter_context(self.nc.sbuf_tensor(name, shape, dtype))

    def build(self):
        nc = self.nc
        dt_ = nc.dram_tensor
        S = self.stage
        self.xwin = dt_("xwin", [WIN, D], F32, kind="ExternalInput").ap()
        self.xres = dt_("xres", [TC, D], F32, kind="ExternalInput").ap()
        self.mv = dt_("mv", [128, 1], F32, kind="ExternalInput").ap()
        self.mv1 = dt_("mv1", [128, 1], F32, kind="ExternalInput").ap()
        self.winT = [dt_(f"winT{l}", [D, 3 * D], F32, kind="ExternalInput").ap() for l in range(3)]
        self.binb = [dt_(f"bin{l}", [128, 24], F32, kind="ExternalInput").ap() for l in range(3)]
        self.binv = [dt_(f"binv{l}", [1, D], F32, kind="ExternalInput").ap() for l in range(3)]
        self.woutT = [dt_(f"woutT{l}", [D, D], F32, kind="ExternalInput").ap() for l in range(3)]
        self.boutb = [dt_(f"bout{l}", [128, 8], F32, kind="ExternalInput").ap() for l in range(3)]
        self.ffg = dt_("ffg", [128, 8], F32, kind="ExternalInput").ap()
        self.ffb = dt_("ffb", [128, 8], F32, kind="ExternalInput").ap()
        self.fc1q = dt_("fc1q", [D, 4096], I32, kind="ExternalInput").ap()
        self.fc1s = dt_("fc1s", [8, 4096], F32, kind="ExternalInput").ap()
        self.fc1b = dt_("fc1b", [128, 32], F32, kind="ExternalInput").ap()
        self.fc2q = dt_("fc2q", [4096, D], I32, kind="ExternalInput").ap()
        self.fc2s = dt_("fc2s", [32, D], F32, kind="ExternalInput").ap()
        self.fc2b = dt_("fc2b", [128, 8], F32, kind="ExternalInput").ap()
        self.wq, self.wk, self.wv, self.wo = [
            dt_(n, [D, D], F32, kind="ExternalInput").ap() for n in ("wq", "wk", "wv", "wo")]
        self.bq, self.bk, self.bv = [
            dt_(n, [1, D], F32, kind="ExternalInput").ap() for n in ("bq", "bk", "bv")]
        self.bo = dt_("bo", [128, 8], F32, kind="ExternalInput").ap()
        self.ew1 = dt_("ew1", [9, 2048, D], F32, kind="ExternalInput").ap()
        self.ew2 = dt_("ew2", [9, D, 2048], F32, kind="ExternalInput").ap()
        self.rwT = dt_("rwT", [D, 8], F32, kind="ExternalInput").ap()
        self.g1 = dt_("g1", [128, 8], F32, kind="ExternalInput").ap()
        self.b1 = dt_("b1", [128, 8], F32, kind="ExternalInput").ap()
        self.ones128 = dt_("ones128", [1, 128], F32, kind="ExternalInput").ap()
        self.onescol_bf = dt_("onescol_bf", [128, 1], BF16, kind="ExternalInput").ap()
        self.on1024_bf = dt_("on1024_bf", [128, 1], BF16, kind="ExternalInput").ap()
        self.idn_bf = dt_("idn_bf", [128, 128], BF16, kind="ExternalInput").ap()
        self.idn_f = dt_("idn_f", [128, 128], F32, kind="ExternalInput").ap()
        self.sel8 = dt_("sel8", [8, 1024], F32, kind="ExternalInput").ap()
        self.sel8h = dt_("sel8h", [128, 64], BF16, kind="ExternalInput").ap()
        self.pw2 = dt_("pw2", [128, 8], F32, kind="ExternalInput").ap()
        self.out = dt_("out", [TC, D], F32, kind="ExternalOutput").ap()
        self.xtm_scr = dt_("xtm_scr", [WIN, D], BF16, kind="Internal").ap()
        self.xbf_scr = dt_("xbf_scr", [D, WIN], BF16, kind="Internal").ap()
        self.l0o_scr = dt_("l0o_scr", [D, 3584], BF16, kind="Internal").ap()
        self.l1o_scr = dt_("l1o_scr", [D, 3072], BF16, kind="Internal").ap()
        self.q_scr = dt_("q_scr", [D, TC], BF16, kind="Internal").ap()
        self.k_scr = dt_("k_scr", [D, TC], BF16, kind="Internal").ap()
        self.v_scr = dt_("v_scr", [TC, D], BF16, kind="Internal").ap()
        self.attn_scr = dt_("attn_scr", [D, TC], BF16, kind="Internal").ap()
        self.h_scr = dt_("h_scr", [D, TC], BF16, kind="Internal").ap()
        self.w1q_scr = dt_("w1q_scr", [2, 2048, D], BF16, kind="Internal").ap()
        self.w2q_scr = dt_("w2q_scr", [2, D, 2048], BF16, kind="Internal").ap()

        self.ones_sb = self.sb("ones_sb", [1, 128], F32)
        self.onescol_sb = self.sb("onescol_sb", [128, 1], BF16)
        self.on1024_sb = self.sb("on1024_sb", [128, 1], BF16)
        self.idnbf_sb = self.sb("idnbf_sb", [128, 128], BF16)
        self.idnf_sb = self.sb("idnf_sb", [128, 128], F32)
        self.mv_sb = self.sb("mv_sb", [128, 1], F32)
        self.sel8_sb = self.sb("sel8_sb", [8, 1024], F32)
        self.sel8h_sb = self.sb("sel8h_sb", [128, 64], BF16)
        self.pw2_sb = self.sb("pw2_sb", [128, 8], F32)
        self.mv1_sb = self.sb("mv1_sb", [128, 1], F32)
        self.row_sb = self.sb("row_sb", [1, 2048], F32)
        self.lnbf = self.sb("lnbf", [128, KD, 512], BF16)
        self.lnbf2 = self.lnbf
        self.scal8 = self.sb("scal8", [128, 8], F32)
        self.mu_sb = self.sb("mu_sb", [1, 512], F32)
        self.std_sb = self.sb("std_sb", [1, 512], F32)
        self.istd_sb = self.sb("istd_sb", [1, 512], F32)

        self.ps_a = self.ctx.enter_context(nc.psum_tensor("ps_a", [128, 2, 512], F32))
        self.ps_b = self.ctx.enter_context(nc.psum_tensor("ps_b", [128, 2, 512], F32))
        self.ps_c = self.ctx.enter_context(nc.psum_tensor("ps_c", [128, 2, 512], F32))
        self.ps_d = self.ctx.enter_context(nc.psum_tensor("ps_d", [128, 2, 512], F32))

        self.p0_consts_x()
        with contextlib.ExitStack() as attn_scope:
            self.alloc_attn(attn_scope)
            self.attn_level(0)
            self.barrier()
            self.attn_level(1)
            self.barrier()
            self.attn_level(2)
            self.barrier()
        self.stream = self.sb("stream", [128, KD, TC], F32)
        with contextlib.ExitStack() as sc:
            self.p3b_outproj_l2(sc)
            self.barrier()
        self.ln_module()
        if S >= 2:
            self.barrier()
            with contextlib.ExitStack() as sc:
                self.p4_l0ffn(sc)
                self.barrier()
            self.ln_module()
        if S >= 3:
            self.barrier()
            with contextlib.ExitStack() as sc:
                self.p5_l1attn(sc)
                self.barrier()
            self.ln_module()
        if S >= 4:
            self.barrier()
            with contextlib.ExitStack() as sc:
                self.p6_l1ffn(sc)
                self.barrier()
            self.ln_module()
        self.barrier()
        self.p7_store()

        blk = self.ctx.enter_context(nc.Block())
        for name in ENGS:
            lst = self.prog[name]
            getattr(blk, name)(lambda e, lst=lst: [f(e) for f in lst])
        self.ctx.close()
        return nc

    def p0_consts_x(self):
        p = self
        for dst_, src_ in (
            (p.ones_sb, p.ones128), (p.onescol_sb, p.onescol_bf),
            (p.on1024_sb, p.on1024_bf), (p.idnbf_sb, p.idn_bf),
            (p.idnf_sb, p.idn_f), (p.mv_sb, p.mv), (p.sel8_sb, p.sel8),
            (p.sel8h_sb, p.sel8h), (p.pw2_sb, p.pw2),
            (p.mv1_sb, p.mv1),
        ):
            self.op("sync", lambda e, d=dst_, s=src_: e.dma_start(d[:, :], s), "c0", 16)
        self.op("gpsimd", lambda e: e.dma_start(p.xtm_scr, p.xwin), "xtm", 16)
        self.wait("sync", "xtm")
        xstg = self.lnbf.ap().bitcast(BF16).rearrange("p a f -> p (a f)")
        for dtl in range(KD):
            self.op("sync", lambda e, d=dtl: e.dma_start(
                xstg[:, 0:WIN], p.xtm_scr[:, d * 128:(d + 1) * 128], transpose=True), "xtt", 16)
            self.wait("sync", "xtt")
            self.op("sync", lambda e, d=dtl: e.dma_start(
                p.xbf_scr[d * 128:(d + 1) * 128, :], xstg[:, 0:WIN]), "xbf", 16)
            self.wait("sync", "xbf")
        self.wait("sync", "c0")
        self.barrier(("sync", "gpsimd", "vector", "scalar", "tensor"))

    def alloc_attn(self, sc):
        p = self
        self.winT_sb = self.sb("winT_sb", [128, KD, 3 * D], BF16, sc)
        self.woutT_sb = self.sb("woutT_sb", [128, KD, D], BF16, sc)
        self.binT_sb = self.sb("binT_sb", [128, 24], F32, sc)
        self.boutT_sb = self.sb("boutT_sb", [128, 8], F32, sc)
        self.vb_bc = self.sb("vb_bc", [128, D], F32, sc)
        self.inb = self.sb("inb", [128, 2, KD, 512], BF16, sc)
        self.pstage = self.sb("pstage", [128, 2, KD, 512], BF16, sc)
        self.q_at = self.sb("q_at", [128, 2, 512], BF16, sc)
        self.k_at = self.sb("k_at", [128, 2, 16, 128], BF16, sc)
        self.v_at = self.sb("v_at", [128, 2, 16, 128], BF16, sc)
        self.Pp = self.sb("Pp", [128, 2, 512], BF16, sc)
        self.attnU = self.sb("attnU", [128, 8, 512], BF16, sc)
        self.attnN = self.sb("attnN", [128, 2, 8, 512], BF16, sc)
        self.sqs = self.sb("sqs", [128, 2, 512], BF16, sc)
        self.m2q_sb = self.sb("m2q_sb", [1, 8, 4], F32, sc)
        self.m2k_sb = self.sb("m2k_sb", [1, 8, 4], F32, sc)
        self.m2_sb = self.sb("m2_sb", [1, 4, 8], F32, sc)
        self.negm_row = self.sb("negm_row", [1, 8], F32, sc)
        self.negm_sb = self.sb("negm_sb", [128, 8], F32, sc)
        self.rden_sb = self.sb("rden_sb", [8, 512], F32, sc)
        self.aT_st = self.sb("aT_st", [128, 2, KD, 512], BF16, sc)

    def level_weights(self, l):
        p = self
        self.op("gpsimd", lambda e: e.dma_start(
            p.winT_sb[:, :, :], p.winT[l].rearrange("(a p) f -> p a f", p=128)), "wlg", 16)
        self.op("gpsimd", lambda e: e.dma_start(
            p.woutT_sb[:, :, :], p.woutT[l].rearrange("(a p) f -> p a f", p=128)), "wlg", 16)
        self.op("sync", lambda e: e.dma_start(p.binT_sb[:, :], p.binb[l]), "wl", 16)
        self.op("sync", lambda e: e.dma_start(p.boutT_sb[:, :], p.boutb[l]), "wl", 16)
        self.op("sync", lambda e: e.dma_start(p.row_sb[0:1, 0:D], p.binv[l]), "wl", 16)
        self.wait("tensor", "wl")
        self.wait("tensor", "wlg")
        for n in range(2):
            self.op("tensor", lambda e, n=n: e.matmul(
                p.ps_c[:, n, :], p.ones_sb[0:1, :], p.row_sb[0:1, n * 512:(n + 1) * 512],
                start=True, stop=True), "t_g" if n == 1 else None, 1)
        self.wait("vector", "t_g")
        for n in range(2):
            self.op("vector", lambda e, n=n: e.tensor_copy(
                p.vb_bc[:, n * 512:(n + 1) * 512], p.ps_c[:, n, :]), "v_g" if n == 1 else None, 1)

    def chunk_input(self, l, j, T):
        pass

    def sub_input(self, l, j, s_i):
        p = self
        par = s_i % 2
        if l == 0:
            src_ = p.xbf_scr
            oa, ob = 448 * j + s_i * 512, None
        elif l == 1:
            src_ = p.l0o_scr
            oa, ob = 960 * j + s_i * 512, 960 * j + 448 + s_i * 512
        else:
            src_ = p.l1o_scr
            oa, ob = s_i * 512, 960 + s_i * 512
        self.wait("sync", "t_pj", self.gc("t_pj"))
        self.op("sync", lambda e, oa=oa, par=par, src_=src_: e.dma_start(
            p.inb[:, par, :, :] if True else None,
            src_.rearrange("(a p) f -> p a f", p=128)[:, :, oa:oa + 512]), "y_in", 16)
        if ob is not None:
            self.op("sync", lambda e, ob=ob, par=par, src_=src_: e.dma_start(
                p.aT_st[:, par, :, :],
                src_.rearrange("(a p) f -> p a f", p=128)[:, :, ob:ob + 512]), "y_in", 16)
            self.wait("vector", "y_in")
            self.op("vector", lambda e, par=par: e.tensor_scalar(
                p.inb[:, par, :, :], p.inb[:, par, :, :], p.mv1_sb[:, 0:1], None,
                ALU.mult), None)
            self.op("vector", lambda e, par=par: e.scalar_tensor_tensor(
                p.inb[:, par, :, :], p.aT_st[:, par, :, :], p.mv_sb[:, 0:1],
                p.inb[:, par, :, :], ALU.mult, ALU.add), "v_mg", 1)
        else:
            self.wait("vector", "y_in")
            self.op("vector", lambda e: e.nop(), "v_mg", 1)

    def chunk_qkv(self, l, j, T):
        p = self
        NS = T // 512
        for s_i in range(NS):
            par = s_i % 2
            self.sub_input(l, j, s_i)
            self.wait("tensor", "v_mg")
            self.wait("tensor", "v_g")
            for qk in range(2):
                for h in range(8):
                    m = qk * 8 + h
                    g = self.gc("t_pj")
                    if g >= 2:
                        self.wait("tensor", "s_pj", g - 1)
                    gp = g % 2
                    for k in range(KD):
                        self.op("tensor", lambda e, m=m, k=k, par=par, gp=gp: e.matmul(
                            p.ps_a[:, gp, :], p.winT_sb[:, k, m * 128:(m + 1) * 128],
                            p.inb[:, par, k, :], start=(k == 0), stop=(k == KD - 1)),
                            "t_pj" if k == KD - 1 else None, 1)
                    self.wait("scalar", "t_pj")
                    if h == 0:
                        self.wait("scalar", "y_p%d" % par, None)
                    self.op("scalar", lambda e, h=h, m=m, gp=gp, par=par: e.activation(
                        p.pstage[:, par, h, :], p.ps_a[:, gp, :], AF.Identity,
                        bias=p.binT_sb[:, m:m + 1]), "s_pj", 1)
                    self.qknorm_tile(h, s_i, from_q=(qk == 0))
                self.wait("sync", "s_pj")
                self.wait("sync", "v_sq")
                dscr = p.q_scr if qk == 0 else p.k_scr
                self.op("sync", lambda e, s_i=s_i, par=par, dscr=dscr: e.dma_start(
                    dscr.rearrange("(a p) f -> p a f", p=128)[:, :, s_i * 512:(s_i + 1) * 512],
                    p.pstage[:, par, :, :]), "y_p%d" % par, 16)
            for tt in range(4):
                for n in range(2):
                    g = self.gc("t_pj")
                    if g >= 2:
                        self.wait("tensor", "s_pj", g - 1)
                    gp = g % 2
                    for k in range(KD):
                        self.op("tensor", lambda e, tt=tt, n=n, k=k, par=par, gp=gp: e.matmul(
                            p.ps_a[:, gp, :], p.inb[:, par, k, tt * 128:(tt + 1) * 128],
                            p.winT_sb[:, k, 2 * D + n * 512:2 * D + (n + 1) * 512],
                            start=(k == 0), stop=(k == KD - 1)), "t_pj" if k == KD - 1 else None, 1)
                    self.wait("vector", "t_pj")
                    if tt == 0 and n == 0:
                        self.wait("vector", "y_p%d" % par, None)
                    pst_v = p.pstage[:, :, :, :].rearrange("q a b f -> q a (b f)")
                    self.op("vector", lambda e, tt=tt, n=n, gp=gp, par=par, pst_v=pst_v: e.tensor_tensor(
                        pst_v[:, par, tt * 1024 + n * 512: tt * 1024 + (n + 1) * 512],
                        p.ps_a[:, gp, :], p.vb_bc[:, n * 512:(n + 1) * 512], ALU.add), "s_pj", 1)
            self.wait("sync", "s_pj")
            pst_v = p.pstage[:, :, :, :].rearrange("q a b f -> q a (b f)")
            self.op("sync", lambda e, s_i=s_i, par=par, pst_v=pst_v: e.dma_start(
                p.v_scr[s_i * 512:(s_i + 1) * 512, :].rearrange("(a p) f -> p a f", p=128),
                pst_v[:, par, :].rearrange("p (a f) -> p a f", a=4)), "y_p%d" % par, 16)
        self.norm_final(T)

    def qknorm_tile(self, h, s_i, from_q):
        p = self
        par = s_i % 2
        sp = self.gc("v_sq") % 2
        self.wait("vector", "s_pj")
        self.wait("vector", "t_n2", self.gc("t_n2"))
        self.op("vector", lambda e, h=h, par=par, sp=sp: e.tensor_tensor(
            p.sqs[:, sp, :], p.pstage[:, par, h, :], p.pstage[:, par, h, :],
            ALU.mult), "v_sq", 1)
        self.wait("tensor", "v_sq")
        npar = self.gc("t_n2") % 2
        self.wait("tensor", "v_n2", max(0, self.gc("v_n2") - 1))
        self.op("tensor", lambda e, sp=sp, npar=npar: e.matmul(
            p.ps_d[0:1, npar, :], p.onescol_sb[:, 0:1], p.sqs[:, sp, :],
            start=True, stop=True), "t_n2", 1)
        self.wait("vector", "t_n2")
        dst = p.m2q_sb if from_q else p.m2k_sb
        self.op("vector", lambda e, h=h, s_i=s_i, npar=npar, dst=dst: e.tensor_reduce(
            dst[0:1, h, s_i:s_i + 1], p.ps_d[0:1, npar, :], AX.X, ALU.max), "v_n2", 1)

    def norm_final(self, T):
        p = self
        NS = T // 512
        self.op("vector", lambda e, NS=NS: e.tensor_reduce(
            p.m2_sb[0:1, 0, :], p.m2q_sb[0:1, :, 0:NS], AX.X, ALU.max), None)
        self.op("vector", lambda e, NS=NS: e.tensor_reduce(
            p.m2_sb[0:1, 1, :], p.m2k_sb[0:1, :, 0:NS], AX.X, ALU.max), None)
        self.op("vector", lambda e: e.tensor_tensor(
            p.m2_sb[0:1, 2, :], p.m2_sb[0:1, 0, :], p.m2_sb[0:1, 1, :], ALU.add), None)
        self.op("vector", lambda e: e.tensor_scalar(
            p.negm_row[0:1, 0:8], p.m2_sb[0:1, 2, :], -0.5 * ISDH, None,
            ALU.mult), "v_m2", 1)
        self.wait("tensor", "v_m2")
        self.op("tensor", lambda e: e.matmul(
            p.ps_c[:, 1, 0:8], p.ones_sb[0:1, :], p.negm_row[0:1, 0:8],
            start=True, stop=True), "t_m3", 1)
        self.wait("vector", "t_m3")
        self.op("vector", lambda e: e.tensor_copy(p.negm_sb[:, 0:8], p.ps_c[:, 1, 0:8]), "v_m3", 1)

    def chunk_attention(self, l, j, T):
        p = self
        NQC = T // 512
        NKT = T // 128
        self.wait("tensor", "v_m3")
        self.wait("sync", "y_p0")
        self.wait("sync", "y_p1")
        for qc in range(NQC):
            qpar = qc % 2
            if qc >= 2:
                self.wait("tensor", "v_rc", self.gc("v_rc"))
            van_base = self.gc("v_an")
            for h in range(8):
                lpar = (qc * 8 + h) % 2
                if qc * 8 + h >= 2:
                    self.wait("sync", "t_pv", self.gc("t_pv"))
                self.op("sync", lambda e, h=h, qc=qc, lpar=lpar: e.dma_start(
                    p.q_at[:, lpar, :],
                    p.q_scr[h * 128:(h + 1) * 128, qc * 512:(qc + 1) * 512]), "y_qa", 16)
                self.op("sync", lambda e, h=h, lpar=lpar, NKT=NKT: e.dma_start(
                    p.k_at[:, lpar, 0:NKT, :],
                    p.k_scr[h * 128:(h + 1) * 128, 0:NKT * 128].rearrange(
                        "p (a f) -> p a f", f=128)), "y_qa", 16)
                self.op("sync", lambda e, h=h, lpar=lpar, NKT=NKT: e.dma_start(
                    p.v_at[:, lpar, 0:NKT, :],
                    p.v_scr[0:NKT * 128, h * 128:(h + 1) * 128].rearrange(
                        "(a p) f -> p a f", p=128)), "y_qa", 16)
                self.wait("tensor", "y_qa")
                g_pv = self.gc("g_pv")
                if g_pv >= 2:
                    self.wait("tensor", "s_u", g_pv - 1)
                self.cnt["g_pv"] = g_pv + 1
                def s_mm(kt):
                    spar = kt % 2
                    self.op("tensor", lambda e, kt=kt, spar=spar, lpar=lpar: e.matmul(
                        p.ps_a[:, spar, :], p.k_at[:, lpar, kt, :],
                        p.q_at[:, lpar, :], start=True, stop=True), "t_s", 1)
                    self.wait("scalar", "t_s")
                    if kt >= 2:
                        self.wait("scalar", "t_pv", self.gc("t_pv") - 1)
                    self.op("scalar", lambda e, h=h, spar=spar: e.activation(
                        p.Pp[:, spar, :], p.ps_a[:, spar, :], AF.Exp,
                        bias=p.negm_sb[:, h:h + 1], scale=ISDH), "s_x", 1)

                def dnpv_mm(kt):
                    spar = kt % 2
                    self.wait("tensor", "s_x", self.sx_base + kt + 1)
                    self.op("tensor", lambda e, h=h, kt=kt, spar=spar, qpar=qpar, NKT=NKT: e.matmul(
                        p.ps_d[0:8, qpar, :], p.sel8h_sb[:, h * 8:(h + 1) * 8],
                        p.Pp[:, spar, :], start=(h == 0 and kt == 0),
                        stop=(h == 7 and kt == NKT - 1)),
                        "t_dn" if (h == 7 and kt == NKT - 1) else None, 1)
                    self.op("tensor", lambda e, kt=kt, spar=spar, lpar=lpar, pvpar=g_pv % 2, NKT=NKT: e.matmul(
                        p.ps_b[:, pvpar, :], p.v_at[:, lpar, kt, :],
                        p.Pp[:, spar, :], start=(kt == 0), stop=(kt == NKT - 1)), "t_pv", 1)

                self.sx_base = self.gc("s_x")
                s_mm(0)
                for kt in range(1, NKT):
                    s_mm(kt)
                    dnpv_mm(kt - 1)
                dnpv_mm(NKT - 1)
                self.wait("scalar", "t_pv")
                self.wait("scalar", "v_an", van_base)
                self.op("scalar", lambda e, h=h, pvpar=g_pv % 2: e.activation(
                    p.attnU[:, h, :], p.ps_b[:, pvpar, :], AF.Copy), "s_u", 1)
            self.wait("vector", "t_dn")
            self.op("vector", lambda e, qpar=qpar: e.reciprocal(
                p.rden_sb[:, :], p.ps_d[0:8, qpar, :]), "v_rc", 1)
            for h in range(8):
                bp = h % 2
                self.wait("tensor", "v_rc")
                if h >= 2:
                    self.wait("tensor", "v_an", self.gc("v_an"))
                else:
                    self.wait("tensor", "v_an", van_base)
                self.op("tensor", lambda e, h=h, bp=bp: e.matmul(
                    p.ps_c[:, bp, :], p.sel8_sb[:, h * 128:(h + 1) * 128],
                    p.rden_sb[:, :], start=True, stop=True), "t_rb", 1)
                self.wait("vector", "t_rb")
                self.wait("vector", "s_u")
                if qc >= 2 and h == 0:
                    self.wait("vector", "y_an%d" % qpar, None)
                self.op("vector", lambda e, h=h, bp=bp, qpar=qpar: e.tensor_tensor(
                    p.attnN[:, qpar, h, :], p.attnU[:, h, :], p.ps_c[:, bp, :],
                    ALU.mult), "v_an", 1)
            self.wait("sync", "v_an")
            self.op("sync", lambda e, qc=qc, qpar=qpar: e.dma_start(
                p.attn_scr.rearrange("(a p) f -> p a f", p=128)[:, :, qc * 512:(qc + 1) * 512],
                p.attnN[:, qpar, :, :]), "y_an%d" % qpar, 16)

    def chunk_outproj(self, l, j, T):
        p = self
        NS = T // 512
        dst = p.l0o_scr if l == 0 else p.l1o_scr
        base = 512 * j * 1 if l == 0 else 1024 * j
        for s_i in range(NS):
            apar = s_i % 2
            self.wait("sync", "y_an0")
            self.wait("sync", "y_an1")
            if s_i >= 2:
                self.wait("sync", "t_o", self.gc("t_o"))
            self.op("sync", lambda e, s_i=s_i, apar=apar: e.dma_start(
                p.aT_st[:, apar, :, :],
                p.attn_scr.rearrange("(a p) f -> p a f", p=128)[:, :, s_i * 512:(s_i + 1) * 512]),
                "y_at", 16)
            self.wait("tensor", "y_at")
            for dtl in range(8):
                g = self.gc("t_o")
                if g >= 2:
                    self.wait("tensor", "s_o", g - 1)
                par = g % 2
                for k in range(KD):
                    self.op("tensor", lambda e, dtl=dtl, k=k, apar=apar, par=par: e.matmul(
                        p.ps_a[:, par, :], p.woutT_sb[:, k, dtl * 128:(dtl + 1) * 128],
                        p.aT_st[:, apar, k, :], start=(k == 0), stop=(k == KD - 1)),
                        "t_o" if k == KD - 1 else None, 1)
                self.wait("scalar", "t_o")
                if self.gc("s_o") >= 2 and dtl == 0:
                    self.wait("scalar", "y_o%d" % apar, None)
                self.op("scalar", lambda e, dtl=dtl, apar=apar, par=par: e.activation(
                    p.attnN[:, apar, dtl, :], p.ps_a[:, par, :], AF.Identity,
                    bias=p.boutT_sb[:, dtl:dtl + 1]), "s_o", 1)
            self.wait("sync", "s_o")
            self.op("sync", lambda e, s_i=s_i, apar=apar, dst=dst, base=base: e.dma_start(
                dst.rearrange("(a p) f -> p a f", p=128)[:, :, base + s_i * 512:base + (s_i + 1) * 512],
                p.attnN[:, apar, :, :]), "y_o%d" % apar, 16)
        if l == 0:
            self.op("sync", lambda e: e.nop(), "y_l0", 1)

    def attn_level(self, l):
        T = (512, 1024, 2048)[l]
        nch = (7, 3, 1)[l]
        self.level_weights(l)
        for j in range(nch):
            self.chunk_qkv(l, j, T)
            self.chunk_attention(l, j, T)
            if l < 2:
                self.chunk_outproj(l, j, T)
            self.barrier()

    def p3b_outproj_l2(self, sc):
        p = self
        self.woutT2s = self.sb("woutT2_sb", [128, KD, D], BF16, sc)
        self.boutT2s = self.sb("boutT2_sb", [128, 8], F32, sc)
        self.aT2_st = self.sb("aT2_st", [128, 2, KD, 512], BF16, sc)
        self.xrs_tm = self.sb("xrs_tm", [128, 2, D], F32, sc)
        self.op("gpsimd", lambda e: e.dma_start(
            p.woutT2s[:, :, :], p.woutT[2].rearrange("(a p) f -> p a f", p=128)), "wlg", 16)
        self.op("sync", lambda e: e.dma_start(p.boutT2s[:, :], p.boutb[2]), "wl", 16)
        for tt in range(16):
            par = tt % 2
            if tt >= 2:
                self.wait("sync", "v_xi", self.gc("v_xi"))
            self.op("sync", lambda e, tt=tt, par=par: e.dma_start(
                p.xrs_tm[:, par, :], p.xres[tt * 128:(tt + 1) * 128, :]), "y_xr", 16)
            self.wait("tensor", "y_xr")
            for dtl in range(8):
                bp = (tt * 8 + dtl) % 2
                if tt * 8 + dtl >= 2:
                    self.wait("tensor", "v_xi", self.gc("v_xi") - 1)
                self.op("tensor", lambda e, tt=tt, dtl=dtl, par=par, bp=bp: e.transpose(
                    p.ps_b[:, bp, 0:128], p.xrs_tm[:, par, dtl * 128:(dtl + 1) * 128],
                    p.idnf_sb[:, :]), "t_xi", 1)
                self.wait("vector", "t_xi")
                self.op("vector", lambda e, tt=tt, dtl=dtl, bp=bp: e.tensor_copy(
                    p.stream[:, dtl, tt * 128:(tt + 1) * 128], p.ps_b[:, bp, 0:128]), "v_xi", 1)
        self.wait("tensor", "wl")
        self.wait("tensor", "wlg")
        for s_i in range(4):
            apar = s_i % 2
            if s_i >= 2:
                self.wait("sync", "t_o2", self.gc("t_o2"))
            self.op("sync", lambda e, s_i=s_i, apar=apar: e.dma_start(
                p.aT2_st[:, apar, :, :],
                p.attn_scr.rearrange("(a p) f -> p a f", p=128)[:, :, s_i * 512:(s_i + 1) * 512]),
                "y_at2", 16)
            self.wait("tensor", "y_at2")
            self.wait("tensor", "v_xi")
            for dtl in range(8):
                g = self.gc("t_o2")
                if g >= 2:
                    self.wait("tensor", "v_o2", g - 1)
                par = g % 2
                for k in range(KD):
                    self.op("tensor", lambda e, dtl=dtl, k=k, apar=apar, par=par: e.matmul(
                        p.ps_a[:, par, :], p.woutT2s[:, k, dtl * 128:(dtl + 1) * 128],
                        p.aT2_st[:, apar, k, :], start=(k == 0), stop=(k == KD - 1)),
                        "t_o2" if k == KD - 1 else None, 1)
                self.wait("vector", "t_o2")
                self.op("vector", lambda e, dtl=dtl, s_i=s_i, par=par: e.scalar_tensor_tensor(
                    p.stream[:, dtl, s_i * 512:(s_i + 1) * 512], p.ps_a[:, par, :],
                    p.boutT2s[:, dtl:dtl + 1],
                    p.stream[:, dtl, s_i * 512:(s_i + 1) * 512], ALU.add, ALU.add), "v_o2", 1)

    def ln_module(self):
        p = self
        for c in range(4):
            cs = slice(c * 512, (c + 1) * 512)
            par = c % 2
            for k in range(KD):
                self.op("vector", lambda e, k=k, cs=cs: e.tensor_copy(
                    p.lnbf[:, k, :], p.stream[:, k, cs]), "v_lc" if k == KD - 1 else None, 1)
            self.wait("tensor", "v_lc")
            if c >= 2:
                self.wait("tensor", "v_ls", self.gc("v_ls"))
            for k in range(KD):
                self.op("tensor", lambda e, k=k, par=par: e.matmul(
                    p.ps_d[0:1, par, :], p.on1024_sb[:, 0:1], p.lnbf[:, k, :],
                    start=(k == 0), stop=(k == KD - 1)), "t_lm" if k == KD - 1 else None, 1)
            self.wait("scalar", "t_lm")
            self.op("scalar", lambda e, par=par: e.copy(
                p.mu_sb[0:1, :], p.ps_d[0:1, par, :]), "s_lm", 1)
            self.wait("tensor", "s_lm")
            self.op("tensor", lambda e, par=par: e.matmul(
                p.ps_c[:, par, :], p.ones_sb[0:1, :], p.mu_sb[0:1, :],
                start=True, stop=True), "t_lb", 1)
            self.wait("vector", "t_lb")
            for k in range(KD):
                self.op("vector", lambda e, k=k, cs=cs, par=par: e.tensor_tensor(
                    p.stream[:, k, cs], p.stream[:, k, cs], p.ps_c[:, par, :],
                    ALU.subtract), "v_ls" if k == KD - 1 else None, 1)
            self.wait("scalar", "v_ls")
            for k in range(KD):
                self.op("scalar", lambda e, k=k, cs=cs: e.activation(
                    p.lnbf2[:, k, :], p.stream[:, k, cs], AF.Square), "s_lq" if k == KD - 1 else None, 1)
            self.wait("tensor", "s_lq")
            for k in range(KD):
                self.op("tensor", lambda e, k=k, par=par: e.matmul(
                    p.ps_d[0:1, par, :], p.on1024_sb[:, 0:1], p.lnbf2[:, k, :],
                    start=(k == 0), stop=(k == KD - 1)), "t_lv" if k == KD - 1 else None, 1)
            self.wait("vector", "t_lv")
            self.op("vector", lambda e, par=par: e.tensor_scalar(
                p.std_sb[0:1, :], p.ps_d[0:1, par, :], EPS, None, ALU.add), "v_lv", 1)
            self.wait("scalar", "v_lv")
            self.op("scalar", lambda e: e.activation(
                p.std_sb[0:1, :], p.std_sb[0:1, :], AF.Sqrt), "s_lv", 1)
            self.wait("vector", "s_lv")
            self.op("vector", lambda e: e.reciprocal(p.istd_sb[0:1, :], p.std_sb[0:1, :]), "v_li", 1)
            self.wait("tensor", "v_li")
            self.op("tensor", lambda e, par=par: e.matmul(
                p.ps_c[:, par, :], p.ones_sb[0:1, :], p.istd_sb[0:1, :],
                start=True, stop=True), "t_li", 1)
            self.wait("vector", "t_li")
            for k in range(KD):
                self.op("vector", lambda e, k=k, cs=cs, par=par: e.tensor_tensor(
                    p.stream[:, k, cs], p.stream[:, k, cs], p.ps_c[:, par, :],
                    ALU.mult), "v_ln" if k == KD - 1 else None, 1)

    def p4_l0ffn(self, sc):
        p = self
        self.ff_hst = self.sb("ff_hst", [128, 1, KD, 512], BF16, sc)
        self.ff_w1 = self.sb("ff_w1", [128, KD, 2048], BF16, sc)
        self.ff_w2 = self.sb("ff_w2", [128, 16, 1024], BF16, sc)
        self.ff_mid = self.sb("ff_mid", [128, 16, 512], BF16, sc)
        self.ff_qint = self.sb("ff_qint", [128, 2, 2048], I32, sc)
        self.ff_db = self.sb("ff_db", [128, KD, 512], BF16, sc)
        self.ff_g = self.sb("ff_g", [128, 8], F32, sc)
        self.ff_b = self.sb("ff_b", [128, 8], F32, sc)
        self.ff_b1 = self.sb("ff_b1", [128, 32], F32, sc)
        self.ff_b2 = self.sb("ff_b2", [128, 8], F32, sc)
        self.ff_b2h = self.sb("ff_b2h", [128, 8], F32, sc)
        for dst_, src_ in ((p.ff_g, p.ffg), (p.ff_b, p.ffb), (p.ff_b1, p.fc1b), (p.ff_b2, p.fc2b)):
            self.op("sync", lambda e, d=dst_, s=src_: e.dma_start(d[:, :], s), "f_p", 16)
        self.wait("vector", "f_p")
        self.op("vector", lambda e: e.tensor_scalar(
            p.ff_b2h[:, :], p.ff_b2[:, :], 0.5, None, ALU.mult), "f_bh", 1)
        for c in range(4):
            cs = slice(c * 512, (c + 1) * 512)
            par = c % 2
            for k in range(KD):
                self.op("vector", lambda e, k=k, cs=cs: e.tensor_copy(
                    p.lnbf[:, k, :], p.stream[:, k, cs]), "fl_c" if k == KD - 1 else None, 1)
            self.wait("tensor", "fl_c")
            if c >= 2:
                self.wait("tensor", "fl_d", self.gc("fl_d"))
            for k in range(KD):
                self.op("tensor", lambda e, k=k, par=par: e.matmul(
                    p.ps_d[0:1, par, :], p.on1024_sb[:, 0:1], p.lnbf[:, k, :],
                    start=(k == 0), stop=(k == KD - 1)), "fl_m" if k == KD - 1 else None, 1)
            self.wait("scalar", "fl_m")
            self.op("scalar", lambda e, par=par: e.copy(
                p.mu_sb[0:1, :], p.ps_d[0:1, par, :]), "fl_s", 1)
            self.wait("tensor", "fl_s")
            self.op("tensor", lambda e, par=par: e.matmul(
                p.ps_c[:, par, :], p.ones_sb[0:1, :], p.mu_sb[0:1, :],
                start=True, stop=True), "fl_b", 1)
            self.wait("vector", "fl_b")
            for k in range(KD):
                self.op("vector", lambda e, k=k, cs=cs, par=par: e.tensor_tensor(
                    p.ff_db[:, k, :], p.stream[:, k, cs], p.ps_c[:, par, :],
                    ALU.subtract), "fl_d" if k == KD - 1 else None, 1)
            self.wait("scalar", "fl_d")
            for k in range(KD):
                self.op("scalar", lambda e, k=k: e.activation(
                    p.lnbf[:, k, :], p.ff_db[:, k, :], AF.Square), "fl_q" if k == KD - 1 else None, 1)
            self.wait("tensor", "fl_q")
            for k in range(KD):
                self.op("tensor", lambda e, k=k, par=par: e.matmul(
                    p.ps_d[0:1, par, :], p.on1024_sb[:, 0:1], p.lnbf[:, k, :],
                    start=(k == 0), stop=(k == KD - 1)), "fl_v" if k == KD - 1 else None, 1)
            self.wait("vector", "fl_v")
            self.op("vector", lambda e, par=par: e.tensor_scalar(
                p.std_sb[0:1, :], p.ps_d[0:1, par, :], EPS, None, ALU.add), "fl_e", 1)
            self.wait("scalar", "fl_e")
            self.op("scalar", lambda e: e.activation(
                p.std_sb[0:1, :], p.std_sb[0:1, :], AF.Sqrt), "fl_sq", 1)
            self.wait("vector", "fl_sq")
            self.op("vector", lambda e: e.reciprocal(p.istd_sb[0:1, :], p.std_sb[0:1, :]), "fl_i", 1)
            self.wait("tensor", "fl_i")
            self.op("tensor", lambda e, par=par: e.matmul(
                p.ps_c[:, par, :], p.ones_sb[0:1, :], p.istd_sb[0:1, :],
                start=True, stop=True), "fl_ib", 1)
            self.wait("vector", "fl_ib")
            self.wait("vector", "y_fh0", None)
            for k in range(KD):
                self.op("vector", lambda e, k=k, par=par: e.tensor_tensor(
                    p.ff_db[:, k, :], p.ff_db[:, k, :], p.ps_c[:, par, :],
                    ALU.mult), None)
                self.op("vector", lambda e, k=k, par=par: e.tensor_scalar(
                    p.ff_hst[:, 0, k, :], p.ff_db[:, k, :], p.ff_g[:, k:k + 1],
                    p.ff_b[:, k:k + 1], ALU.mult, ALU.add), "fl_h" if k == KD - 1 else None, 1)
            self.wait("sync", "fl_h")
            self.op("sync", lambda e, c=c, par=par: e.dma_start(
                p.h_scr.rearrange("(a q) f -> q a f", q=128)[:, :, c * 512:(c + 1) * 512],
                p.ff_hst[:, 0, :, :]), "y_fh0", 16)
        self.qints = 0
        for fh in range(2):
            for g in range(KD):
                qpar = self.qints % 2
                self.qints += 1
                sq = "y_fq" if qpar == 0 else "y_fq1"
                self.wait("sync", "v_dq", max(0, self.gc("v_dq") - 4))
                self.op("sync", lambda e, g=g, fh=fh, qpar=qpar: e.dma_start(
                    p.ff_qint[:, qpar, :], p.fc1q[g * 128:(g + 1) * 128,
                                                  fh * 2048:(fh + 1) * 2048]), sq, 16)
                self.wait("sync", "v_dq", self.gc("v_dq"))
                self.op("sync", lambda e, g=g, fh=fh: e.dma_start(
                    p.row_sb[0:1, :], p.fc1s[g:g + 1, fh * 2048:(fh + 1) * 2048]), "y_fs", 16)
                self.wait("tensor", sq)
                self.wait("tensor", "y_fs")
                if fh == 1 and g == 0:
                    self.wait("tensor", "t_f1", self.gc("t_f1"))
                for n in range(4):
                    bp = n % 2
                    if n >= 2 or g > 0 or fh > 0:
                        self.wait("tensor", "v_dq", self.gc("v_dq") - 1)
                    self.op("tensor", lambda e, n=n, bp=bp: e.matmul(
                        p.ps_c[:, bp, :], p.ones_sb[0:1, :], p.row_sb[0:1, n * 512:(n + 1) * 512],
                        start=True, stop=True), "t_db", 1)
                    self.wait("vector", "t_db")
                    self.op("vector", lambda e, g=g, n=n, bp=bp, qpar=qpar: e.scalar_tensor_tensor(
                        p.ff_w1[:, g, n * 512:(n + 1) * 512],
                        p.ff_qint[:, qpar, n * 512:(n + 1) * 512], -128.0,
                        p.ps_c[:, bp, :], ALU.add, ALU.mult), "v_dq", 1)
            for g2 in range(16):
                qpar = self.qints % 2
                self.qints += 1
                sq = "y_fq" if qpar == 0 else "y_fq1"
                self.wait("sync", "v_dq", max(0, self.gc("v_dq") - 2))
                self.op("sync", lambda e, g2=g2, fh=fh, qpar=qpar: e.dma_start(
                    p.ff_qint[:, qpar, 0:1024], p.fc2q[fh * 2048 + g2 * 128:fh * 2048 + (g2 + 1) * 128, :]),
                    sq, 16)
                self.wait("sync", "v_dq", self.gc("v_dq"))
                self.op("sync", lambda e, g2=g2, fh=fh: e.dma_start(
                    p.row_sb[0:1, 0:1024], p.fc2s[fh * 16 + g2:fh * 16 + g2 + 1, :]), "y_fs", 16)
                self.wait("tensor", sq)
                self.wait("tensor", "y_fs")
                if fh == 1 and g2 == 0:
                    self.wait("tensor", "t_f2", self.gc("t_f2"))
                for n in range(2):
                    bp = n % 2
                    self.wait("tensor", "v_dq", self.gc("v_dq") - 1)
                    self.op("tensor", lambda e, n=n, bp=bp: e.matmul(
                        p.ps_c[:, bp, :], p.ones_sb[0:1, :], p.row_sb[0:1, n * 512:(n + 1) * 512],
                        start=True, stop=True), "t_db", 1)
                    self.wait("vector", "t_db")
                    self.op("vector", lambda e, g2=g2, n=n, bp=bp, qpar=qpar: e.scalar_tensor_tensor(
                        p.lnbf[:, 0:4, :].rearrange("q a f -> q (a f)")[:, n * 512:(n + 1) * 512],
                        p.ff_qint[:, qpar, n * 512:(n + 1) * 512], -128.0,
                        p.ps_c[:, bp, :], ALU.add, ALU.mult), None)
                    self.op("vector", lambda e, g2=g2, n=n: e.tensor_scalar(
                        p.ff_w2[:, g2, n * 512:(n + 1) * 512],
                        p.lnbf[:, 0:4, :].rearrange("q a f -> q (a f)")[:, n * 512:(n + 1) * 512],
                        0.5, None, ALU.mult), "v_dq", 1)
            for c in range(4):
                hpar = 0
                self.wait("sync", "y_fh0")
                if fh > 0 or c >= 2:
                    self.wait("sync", "t_f1", self.gc("t_f1"))
                self.wait("sync", "t_f1", self.gc("t_f1"))
                self.op("sync", lambda e, c=c: e.dma_start(
                    p.ff_hst[:, 0, :, :],
                    p.h_scr.rearrange("(a q) f -> q a f", q=128)[:, :, c * 512:(c + 1) * 512]),
                    "y_fc", 16)
                self.wait("tensor", "y_fc")
                self.wait("tensor", "v_dq")
                for ft in range(16):
                    g = self.gc("t_f1")
                    gp = g % 2
                    if g >= 2:
                        self.wait("tensor", "s_f1", g - 1)
                    for k in range(KD):
                        self.op("tensor", lambda e, ft=ft, k=k, gp=gp: e.matmul(
                            p.ps_a[:, gp, :], p.ff_w1[:, k, ft * 128:(ft + 1) * 128],
                            p.ff_hst[:, 0, k, :], start=(k == 0), stop=(k == KD - 1)),
                            "t_f1" if k == KD - 1 else None, 1)
                    self.wait("scalar", "t_f1")
                    if ft == 0:
                        self.wait("scalar", "t_f2", self.gc("t_f2"))
                    self.op("scalar", lambda e, ft=ft, fh=fh, gp=gp: e.activation(
                        p.ff_mid[:, ft, :], p.ps_a[:, gp, :], AF.Silu,
                        bias=p.ff_b1[:, fh * 16 + ft:fh * 16 + ft + 1]), "s_f1", 1)
                for dt_ in range(8):
                    g = self.gc("t_f2")
                    gp = g % 2
                    if g >= 2:
                        self.wait("tensor", "v_f2", g - 1)
                    if dt_ == 0:
                        self.wait("tensor", "s_f1", self.gc("s_f1"))
                    for kf in range(16):
                        self.op("tensor", lambda e, dt_=dt_, kf=kf, gp=gp: e.matmul(
                            p.ps_b[:, gp, :], p.ff_w2[:, kf, dt_ * 128:(dt_ + 1) * 128],
                            p.ff_mid[:, kf, :], start=(kf == 0), stop=(kf == 15)),
                            "t_f2" if kf == 15 else None, 1)
                    self.wait("vector", "t_f2")
                    if fh == 0:
                        self.op("vector", lambda e, dt_=dt_, c=c, gp=gp: e.tensor_tensor(
                            p.stream[:, dt_, c * 512:(c + 1) * 512],
                            p.stream[:, dt_, c * 512:(c + 1) * 512], p.ps_b[:, gp, :],
                            ALU.add), "v_f2", 1)
                    else:
                        self.op("vector", lambda e, dt_=dt_, c=c, gp=gp: e.scalar_tensor_tensor(
                            p.stream[:, dt_, c * 512:(c + 1) * 512],
                            p.ps_b[:, gp, :], p.ff_b2h[:, dt_:dt_ + 1],
                            p.stream[:, dt_, c * 512:(c + 1) * 512], ALU.add, ALU.add), "v_f2", 1)


    def p5_l1attn(self, sc):
        p = self
        self.a5_w4 = self.sb("a5_w4", [128, KD, 4096], BF16, sc)
        self.a5_bb = self.sb("a5_bb", [128, 3, 1024], BF16, sc)
        self.a5_bo = self.sb("a5_bo", [128, 8], F32, sc)
        self.a5_xc = self.lnbf
        self.a5_qkv = self.sb("a5_qkv", [128, 3, 4, 1024], BF16, sc)
        self.a5_prod = self.sb("a5_prod", [128, 1, 1024], BF16, sc)
        self.a5_S = self.sb("a5_S", [128, 256], F32, sc)
        self.a5_P = self.sb("a5_P", [128, 256], F32, sc)
        self.a5_nm = self.sb("a5_nm", [128, 16], F32, sc)
        self.a5_dn = self.sb("a5_dn", [128, 16], F32, sc)
        self.a5_rc = self.sb("a5_rc", [128, 16], F32, sc)
        self.a5_at = self.sb("a5_at", [128, 4, 1024], BF16, sc)
        self.a5_aT = self.sb("a5_aT", [128, KD, 512], BF16, sc)
        for i, w in enumerate((p.wq, p.wk, p.wv, p.wo)):
            self.op("gpsimd", lambda e, i=i, w=w: e.dma_start(
                p.a5_w4[:, :, i * 1024:(i + 1) * 1024],
                w.rearrange("(a q) f -> q a f", q=128)), "g5_w", 16)
        self.op("sync", lambda e: e.dma_start(p.a5_bo[:, :], p.bo), "y5_b", 16)
        for i, b in enumerate((p.bq, p.bk, p.bv)):
            self.wait("sync", "v5_bb", self.gc("v5_bb"))
            self.op("sync", lambda e, b=b: e.dma_start(p.row_sb[0:1, 0:1024], b), "y5_b", 16)
            self.wait("tensor", "y5_b")
            for n in range(2):
                bp = n % 2
                self.wait("tensor", "v5_bb", self.gc("v5_bb"))
                self.op("tensor", lambda e, n=n, bp=bp: e.matmul(
                    p.ps_c[:, bp, :], p.ones_sb[0:1, :], p.row_sb[0:1, n * 512:(n + 1) * 512],
                    start=True, stop=True), "t5_bb", 1)
                self.wait("vector", "t5_bb")
                self.op("vector", lambda e, i=i, n=n, bp=bp: e.tensor_copy(
                    p.a5_bb[:, i, n * 512:(n + 1) * 512], p.ps_c[:, bp, :]), "v5_bb", 1)
        self.wait("tensor", "g5_w")
        self.wait("vector", "v5_bb")
        for c in range(4):
            cs = slice(c * 512, (c + 1) * 512)
            self.wait("vector", "t5_p", self.gc("t5_p"))
            for k in range(KD):
                self.op("vector", lambda e, k=k, cs=cs: e.tensor_copy(
                    p.a5_xc[:, k, :], p.stream[:, k, cs]), "v5_x" if k == KD - 1 else None, 1)
            self.wait("tensor", "v5_x")
            for i in range(3):
                for tt in range(4):
                    for n in range(2):
                        g = self.gc("t5_p")
                        gp = g % 2
                        if g >= 2:
                            self.wait("tensor", "v5_p", g - 1)
                        for k in range(KD):
                            self.op("tensor", lambda e, i=i, tt=tt, n=n, k=k, gp=gp: e.matmul(
                                p.ps_a[:, gp, :], p.a5_xc[:, k, tt * 128:(tt + 1) * 128],
                                p.a5_w4[:, k, i * 1024 + n * 512:i * 1024 + (n + 1) * 512],
                                start=(k == 0), stop=(k == KD - 1)), "t5_p" if k == KD - 1 else None, 1)
                        self.wait("vector", "t5_p")
                        self.op("vector", lambda e, i=i, tt=tt, n=n, gp=gp: e.tensor_tensor(
                            p.a5_qkv[:, i, tt, n * 512:(n + 1) * 512], p.ps_a[:, gp, :],
                            p.a5_bb[:, i, n * 512:(n + 1) * 512], ALU.add), "v5_p", 1)
            self.wait("vector", "v5_p")
            self.wait("vector", "t5_t", self.gc("t5_t"))
            for tt in range(4):
                qv = p.a5_qkv
                for h in range(16):
                    pp = 0
                    self.op("vector", lambda e, tt=tt, h=h, pp=pp, qv=qv: e.tensor_tensor(
                        p.a5_prod[:, pp, :].rearrange("q (a b) -> q a b", a=16),
                        qv[:, 0, tt, h * 64:(h + 1) * 64].unsqueeze(1).broadcast_to([128, 16, 64]),
                        qv[:, 1, tt, :].rearrange("q (a b) -> q a b", a=16),
                        ALU.mult), None)
                    self.op("vector", lambda e, h=h, pp=pp: e.tensor_reduce(
                        p.a5_S[:, h * 16:(h + 1) * 16],
                        p.a5_prod[:, pp, :].rearrange("q (a b) -> q a b", a=16),
                        AX.X, ALU.add), None)
                self.op("vector", lambda e: e.tensor_reduce(
                    p.a5_nm[:, :], p.a5_S[:, :].rearrange("q (a b) -> q a b", a=16),
                    AX.X, ALU.max, negate=True), None)
                for h in range(16):
                    self.op("vector", lambda e, h=h: e.tensor_scalar(
                        p.a5_S[:, h * 16:(h + 1) * 16], p.a5_S[:, h * 16:(h + 1) * 16],
                        p.a5_nm[:, h:h + 1], None, ALU.add),
                        "v5_s" if h == 15 else None, 1)
                self.wait("scalar", "v5_s")
                self.op("scalar", lambda e: e.activation(
                    p.a5_P[:, :], p.a5_S[:, :], AF.Exp, scale=0.125), "s5_x", 1)
                self.wait("vector", "s5_x")
                self.op("vector", lambda e: e.tensor_reduce(
                    p.a5_dn[:, :], p.a5_P[:, :].rearrange("q (a b) -> q a b", a=16),
                    AX.X, ALU.add), "v5_dn", 1)
                self.wait("scalar", "v5_dn")
                self.op("scalar", lambda e: e.activation(
                    p.a5_rc[:, :], p.a5_dn[:, :], AF.Ln), "s5_l", 1)
                self.wait("vector", "s5_l")
                self.op("vector", lambda e: e.tensor_scalar(
                    p.a5_rc[:, :], p.a5_rc[:, :], -1.0, None, ALU.mult), "v5_ng", 1)
                self.wait("scalar", "v5_ng")
                self.op("scalar", lambda e: e.activation(
                    p.a5_rc[:, :], p.a5_rc[:, :], AF.Exp), "s5_e", 1)
                self.wait("vector", "s5_e")
                for h in range(16):
                    self.op("vector", lambda e, h=h: e.tensor_scalar(
                        p.a5_P[:, h * 16:(h + 1) * 16], p.a5_P[:, h * 16:(h + 1) * 16],
                        p.a5_rc[:, h:h + 1], None, ALU.mult), None)
                for h in range(16):
                    self.op("vector", lambda e, tt=tt, h=h, qv=qv: e.tensor_scalar(
                        p.a5_at[:, tt, h * 64:(h + 1) * 64], qv[:, 2, tt, 0:64],
                        p.a5_P[:, h * 16:h * 16 + 1], None, ALU.mult), None)
                    for e8 in range(1, 16):
                        self.op("vector", lambda e, tt=tt, h=h, e8=e8, qv=qv: e.scalar_tensor_tensor(
                            p.a5_at[:, tt, h * 64:(h + 1) * 64],
                            qv[:, 2, tt, e8 * 64:(e8 + 1) * 64],
                            p.a5_P[:, h * 16 + e8:h * 16 + e8 + 1],
                            p.a5_at[:, tt, h * 64:(h + 1) * 64], ALU.mult, ALU.add),
                            "v5_pv" if (h == 15 and e8 == 15) else None, 1)
            self.wait("tensor", "v5_pv")
            for tt in range(4):
                for dtl in range(8):
                    g = self.gc("t5_t")
                    bp = g % 2
                    if g >= 2:
                        self.wait("tensor", "s5_t", g - 1)
                    def _tp(e, tt=tt, dtl=dtl, bp=bp):
                        with p.nc.allow_low_precision(reason="bf16 transpose passthrough"):
                            return e.transpose(
                                p.ps_c[:, bp, 0:64].bitcast(BF16),
                                p.a5_at[:, tt, dtl * 128:(dtl + 1) * 128], p.idnbf_sb[:, :])
                    self.op("tensor", _tp, "t5_t", 1)
                    self.wait("scalar", "t5_t")
                    if g == 0 and c > 0:
                        self.wait("scalar", "t5_o", self.gc("t5_o"))
                    self.op("scalar", lambda e, tt=tt, dtl=dtl, bp=bp: e.activation(
                        p.a5_aT[:, dtl, tt * 128:(tt + 1) * 128], p.ps_c[:, bp, 0:64].bitcast(BF16),
                        AF.Copy), "s5_t", 1)
            self.wait("tensor", "s5_t")
            for dtl in range(8):
                g = self.gc("t5_o")
                gp = g % 2
                if g >= 2:
                    self.wait("tensor", "v5_o", g - 1)
                for k in range(KD):
                    self.op("tensor", lambda e, dtl=dtl, k=k, gp=gp: e.matmul(
                        p.ps_b[:, gp, :], p.a5_w4[:, k, 3 * 1024 + dtl * 128:3 * 1024 + (dtl + 1) * 128],
                        p.a5_aT[:, k, :], start=(k == 0), stop=(k == KD - 1)),
                        "t5_o" if k == KD - 1 else None, 1)
                self.wait("vector", "t5_o")
                self.op("vector", lambda e, dtl=dtl, cs=cs, gp=gp: e.scalar_tensor_tensor(
                    p.stream[:, dtl, cs], p.ps_b[:, gp, :], p.a5_bo[:, dtl:dtl + 1],
                    p.stream[:, dtl, cs], ALU.add, ALU.add), "v5_o", 1)


    def p6_l1ffn(self, sc):
        p = self
        self.h6st = self.sb("h6st", [128, 1, KD, 512], BF16, sc)
        self.p6s8 = self.sb("p6s8", [128, 2048], F32, sc)
        self.w1T6 = self.sb("w1T6", [128, KD, 2048], BF16, sc)
        self.w2T6 = self.sb("w2T6", [128, 16, 1024], BF16, sc)
        self.qo6 = self.sb("qo6", [128, 2048], BF16, sc)
        self.mid6 = self.sb("mid6", [128, 16, 512], BF16, sc)
        self.gb6 = self.sb("gb6", [128, 512], BF16, sc)
        self.gateT = self.sb("gateT", [8, 2048], F32, sc)
        self.rwT6 = self.sb("rwT6", [128, KD, 8], F32, sc)
        self.rg6 = self.sb("rg6", [128, 8], F32, sc)
        self.rb6 = self.sb("rb6", [128, 8], F32, sc)
        self.rs6 = self.sb("rs6", [8, 256], F32, sc)
        self.tk6 = self.sb("tk6", [128, 16], F32, sc)
        self.qsc = self.sb("qsc", [128, 4, 8], F32, sc)
        self.sm6 = self.sb("sm6", [128, 8, 8], F32, sc)
        self.g86 = self.sb("g86", [128, 8], F32, sc)
        df32 = self.p6s8[:, :].rearrange("q (a f) -> q a f", a=KD)
        hf32 = df32
        qw = self.p6s8
        self.op("sync", lambda e: e.dma_start(
            p.rwT6[:, :, :], p.rwT.rearrange("(a q) f -> q a f", q=128)), "y6_p", 16)
        self.op("sync", lambda e: e.dma_start(p.rg6[:, :], p.g1), "y6_p", 16)
        self.op("sync", lambda e: e.dma_start(p.rb6[:, :], p.b1), "y6_p", 16)
        self.wait("vector", "y6_p")
        self.wait("tensor", "y6_p")
        self.op("vector", lambda e: e.memset(p.sm6[:, :, :], 0.0), None)
        self.op("vector", lambda e: e.memset(p.qsc[:, :, :], 0.0), "vsp", 1)
        self.wait("vector", "vsp")
        for c8 in range(8):
            cs = slice(c8 * 256, (c8 + 1) * 256)
            par = c8 % 2
            hpar = 0
            for k in range(KD):
                self.op("vector", lambda e, k=k, cs=cs: e.tensor_copy(
                    p.lnbf[:, k, 0:256], p.stream[:, k, cs]), "l6_c" if k == KD - 1 else None, 1)
            self.wait("tensor", "l6_c")
            if c8 >= 2:
                self.wait("tensor", "l6_d", self.gc("l6_d"))
            for k in range(KD):
                self.op("tensor", lambda e, k=k, par=par: e.matmul(
                    p.ps_d[0:1, par, 0:256], p.on1024_sb[:, 0:1], p.lnbf[:, k, 0:256],
                    start=(k == 0), stop=(k == KD - 1)), "l6_m" if k == KD - 1 else None, 1)
            self.wait("scalar", "l6_m")
            self.op("scalar", lambda e, par=par: e.copy(
                p.mu_sb[0:1, 0:256], p.ps_d[0:1, par, 0:256]), "l6_s", 1)
            self.wait("tensor", "l6_s")
            self.op("tensor", lambda e, par=par: e.matmul(
                p.ps_c[:, par, 0:256], p.ones_sb[0:1, :], p.mu_sb[0:1, 0:256],
                start=True, stop=True), "l6_b", 1)
            self.wait("vector", "l6_b")
            self.wait("vector", "t6_r", self.gc("t6_r"))
            for k in range(KD):
                self.op("vector", lambda e, k=k, cs=cs, par=par, df32=df32: e.tensor_tensor(
                    df32[:, k, :], p.stream[:, k, cs], p.ps_c[:, par, 0:256],
                    ALU.subtract), "l6_d" if k == KD - 1 else None, 1)
            self.wait("scalar", "l6_d")
            for k in range(KD):
                self.op("scalar", lambda e, k=k, df32=df32: e.activation(
                    p.lnbf[:, k, 0:256], df32[:, k, :], AF.Square), "l6_q" if k == KD - 1 else None, 1)
            self.wait("tensor", "l6_q")
            for k in range(KD):
                self.op("tensor", lambda e, k=k, par=par: e.matmul(
                    p.ps_d[0:1, par, 0:256], p.on1024_sb[:, 0:1], p.lnbf[:, k, 0:256],
                    start=(k == 0), stop=(k == KD - 1)), "l6_v" if k == KD - 1 else None, 1)
            self.wait("vector", "l6_v")
            self.op("vector", lambda e, par=par: e.tensor_scalar(
                p.std_sb[0:1, 0:256], p.ps_d[0:1, par, 0:256], EPS, None, ALU.add), "l6_e", 1)
            self.wait("scalar", "l6_e")
            self.op("scalar", lambda e: e.activation(
                p.std_sb[0:1, 0:256], p.std_sb[0:1, 0:256], AF.Sqrt), "l6_sq", 1)
            self.wait("vector", "l6_sq")
            self.op("vector", lambda e: e.reciprocal(p.istd_sb[0:1, 0:256], p.std_sb[0:1, 0:256]), "l6_i", 1)
            self.wait("tensor", "l6_i")
            self.op("tensor", lambda e, par=par: e.matmul(
                p.ps_c[:, par, 0:256], p.ones_sb[0:1, :], p.istd_sb[0:1, 0:256],
                start=True, stop=True), "l6_ib", 1)
            self.wait("vector", "l6_ib")
            if c8 >= 2:
                self.wait("vector", "y6_h0", None)
            for k in range(KD):
                self.op("vector", lambda e, k=k, par=par, df32=df32: e.tensor_tensor(
                    df32[:, k, :], df32[:, k, :], p.ps_c[:, par, 0:256],
                    ALU.mult), None)
                self.op("vector", lambda e, k=k, c8=c8, hpar=hpar, df32=df32: e.tensor_scalar(
                    p.h6st[:, hpar, k, (c8 % 2) * 256:(c8 % 2) * 256 + 256],
                    df32[:, k, :], p.rg6[:, k:k + 1],
                    p.rb6[:, k:k + 1], ALU.mult, ALU.add), None)
                self.op("vector", lambda e, k=k, df32=df32, hf32=hf32: e.tensor_scalar(
                    hf32[:, k, 0:256], df32[:, k, :], p.rg6[:, k:k + 1],
                    p.rb6[:, k:k + 1], ALU.mult, ALU.add), "l6_h" if k == KD - 1 else None, 1)
            if c8 % 2 == 1:
                self.wait("sync", "l6_h")
                self.op("sync", lambda e, c8=c8: e.dma_start(
                    p.h_scr.rearrange("(a q) f -> q a f", q=128)[:, :, (c8 // 2) * 512:(c8 // 2 + 1) * 512],
                    p.h6st[:, 0, :, :]), "y6_h0", 16)
            self.wait("tensor", "l6_h")
            if c8 >= 2:
                self.wait("tensor", "s6_rs", self.gc("s6_rs"))
            for k in range(KD):
                self.op("tensor", lambda e, k=k, par=par, hf32=hf32: e.matmul(
                    p.ps_d[0:8, par, 0:256], p.rwT6[:, k, :], hf32[:, k, 0:256],
                    start=(k == 0), stop=(k == KD - 1)), "t6_r" if k == KD - 1 else None, 1)
            self.wait("scalar", "t6_r")
            self.op("scalar", lambda e, par=par: e.copy(
                p.rs6[0:8, :], p.ps_d[0:8, par, 0:256]), "s6_rs", 1)
            for t2 in range(2):
                gtile = c8 * 2 + t2
                bp = gtile % 2
                self.wait("tensor", "s6_rs")
                if gtile >= 2:
                    self.wait("tensor", "v6_st", self.gc("v6_st"))
                self.op("tensor", lambda e, t2=t2, bp=bp: e.transpose(
                    p.ps_b[:, bp, 0:8], p.rs6[0:8, t2 * 128:(t2 + 1) * 128],
                    p.idnf_sb[0:8, 0:8]), "t6_st", 1)
                self.wait("vector", "t6_st")
                self.op("vector", lambda e, bp=bp: e.tensor_copy(
                    p.tk6[:, 0:8], p.ps_b[:, bp, 0:8]), "v6_st", 1)
                self.op("vector", lambda e: e.tensor_reduce(
                    p.sm6[:, 0, 0:1], p.tk6[:, 0:8], AX.X, ALU.max, negate=True),
                    "vsp", 1)
                self.wait("vector", "vsp")
                self.op("vector", lambda e: e.tensor_scalar(
                    p.tk6[:, 0:8], p.tk6[:, 0:8], p.sm6[:, 0, 0:1], None,
                    ALU.add), "v6_sm", 1)
                self.wait("scalar", "v6_sm")
                self.op("scalar", lambda e: e.activation(
                    p.tk6[:, 8:16], p.tk6[:, 0:8], AF.Exp, accum_out=p.sm6[:, 1, 0:1]), "s6_x", 1)
                self.wait("scalar", "s6_x")
                self.op("scalar", lambda e: e.activation(
                    p.sm6[:, 2, 0:1], p.sm6[:, 1, 0:1], AF.Ln), "s6_l", 1)
                self.wait("vector", "s6_l")
                self.op("vector", lambda e: e.tensor_scalar(
                    p.sm6[:, 3, 0:8], p.sm6[:, 2, 0:8], -1.0, None, ALU.mult), "v6_ng", 1)
                self.wait("scalar", "v6_ng")
                self.op("scalar", lambda e: e.activation(
                    p.sm6[:, 4, 0:1], p.sm6[:, 3, 0:1], AF.Exp), "s6_e", 1)
                self.wait("vector", "s6_e")
                self.op("vector", lambda e: e.tensor_scalar(
                    p.tk6[:, 8:16], p.tk6[:, 8:16], p.sm6[:, 4, 0:1], None,
                    ALU.mult), "vsp", 1)
                self.wait("vector", "vsp")
                self.op("vector", lambda e: e.tensor_reduce(
                    p.sm6[:, 5, 0:1], p.tk6[:, 8:16], AX.X, ALU.max), "vsp", 1)
                self.wait("vector", "vsp")
                self.op("vector", lambda e: e.tensor_scalar(
                    p.g86[:, 0:8], p.tk6[:, 8:16], p.sm6[:, 5, 0:1], None,
                    ALU.is_ge), "vsp", 1)
                self.wait("vector", "vsp")
                self.op("vector", lambda e: e.tensor_tensor(
                    p.g86[:, 0:8], p.g86[:, 0:8], p.pw2_sb[:, 0:8], ALU.mult), "vsp", 1)
                self.wait("vector", "vsp")
                self.op("vector", lambda e: e.tensor_reduce(
                    p.sm6[:, 6, 0:1], p.g86[:, 0:8], AX.X, ALU.max), "vsp", 1)
                self.wait("vector", "vsp")
                self.op("vector", lambda e: e.tensor_scalar(
                    p.g86[:, 0:8], p.g86[:, 0:8], p.sm6[:, 6, 0:1], None,
                    ALU.is_ge), "vsp", 1)
                self.wait("vector", "vsp")
                self.op("vector", lambda e: e.tensor_scalar(
                    p.g86[:, 0:8], p.g86[:, 0:8], p.sm6[:, 5, 0:1], None,
                    ALU.mult), "v6_g", 1)
                self.wait("tensor", "v6_g")
                self.op("tensor", lambda e, bp=bp: e.transpose(
                    p.ps_b[0:8, bp, 128:256], p.g86[:, 0:8], p.idnf_sb[:, :]), "t6_g", 1)
                self.wait("vector", "t6_g")
                self.op("vector", lambda e, gtile=gtile, bp=bp: e.tensor_copy(
                    p.gateT[0:8, gtile * 128:(gtile + 1) * 128],
                    p.ps_b[0:8, bp, 128:256]), "v6_st", 1)
        self.wait("tensor", "v6_st")
        self.wait("tensor", "v6_g")
        self.wait("sync", "s6_rs")
        v6g0 = self.gc("v6_g")
        self.q_last = {0: None, 1: None}
        for ex in range(9):
            epar = ex % 2
            tiles = [(0, t, 1024, (t % 2) * 1024, [t % 2]) for t in range(16)] + \
                    [(1, t, 2048, 0, [0, 1]) for t in range(8)]
            su = {s: [jj for jj, td in enumerate(tiles) if s in td[4]] for s in (0, 1)}
            ptr = {0: 0, 1: 0}

            def q_dep(slots):
                dep = {}
                for s in slots:
                    if p.q_last[s]:
                        for kk, vv in p.q_last[s].items():
                            dep[kk] = max(dep.get(kk, 0), vv)
                return dep

            q_inval = {}

            def rec_in(j, s, ex=ex):
                mat, t_i, NF, off, slots = tiles[j]
                g = self.gc("s6_g")
                if g:
                    self.wait("sync", "s6_g", g)
                srcw = p.ew1 if mat == 0 else p.ew2
                c0 = 0 if mat == 0 else s * 1024
                sw = "y6_qw" if s == 0 else "y6_qw2"
                self.op("sync", lambda e, ex=ex, t_i=t_i, s=s, c0=c0, srcw=srcw, qw=qw: e.dma_start(
                    qw[:, s * 1024:(s + 1) * 1024],
                    srcw[ex, t_i * 128:(t_i + 1) * 128, c0:c0 + 1024]), sw, 16)
                q_inval[(j, s)] = (sw, self.gc(sw))

            def adv(jdone):
                for s in (0, 1):
                    while ptr[s] < len(su[s]):
                        if ptr[s] > 0 and su[s][ptr[s] - 1] > jdone:
                            break
                        rec_in(su[s][ptr[s]], s)
                        ptr[s] += 1

            adv(-1)
            for j in range(len(tiles)):
                mat, t_i, NF, off, slots = tiles[j]
                par = slots[0] if len(slots) == 1 else 0
                arow, trow, nrow = 7 - par, 2 * par, 2 * par + 1
                dep = q_dep(slots)
                if dep.get("y6_qo"):
                    self.wait("scalar", "y6_qo", dep["y6_qo"])
                if dep.get("y6_qo2"):
                    self.wait("scalar", "y6_qo2", dep["y6_qo2"])
                if ex == 0 and j == 1:
                    self.wait("scalar", "v6_g", v6g0)
                if mat == 0:
                    sw, v = q_inval[(j, par)]
                    self.wait("scalar", sw, v)
                    self.op("scalar", lambda e, NF=NF, off=off, arow=arow, qw=qw: e.activation(
                        p.qo6[:, off:off + NF], qw[:, off:off + NF], AF.Abs,
                        accum_out=p.sm6[:, arow, 0:1]), "s6_a", 1)
                    self.wait("vector", "s6_a")
                    self.op("vector", lambda e, NF=NF, arow=arow, trow=trow: e.tensor_scalar(
                        p.qsc[:, trow, 0:8], p.sm6[:, arow, 0:8], 0.5 / NF, None,
                        ALU.mult), "vsp", 1)
                else:
                    sw, v = q_inval[(j, 0)]
                    self.wait("scalar", sw, v)
                    self.op("scalar", lambda e, qw=qw: e.activation(
                        p.qo6[:, 0:1024], qw[:, 0:1024], AF.Abs,
                        accum_out=p.sm6[:, 7, 0:1]), "s6_a", 1)
                    sw, v = q_inval[(j, 1)]
                    self.wait("scalar", sw, v)
                    self.wait("scalar", "vsp", self.gc("vsp"))
                    self.op("scalar", lambda e, qw=qw: e.activation(
                        p.qo6[:, 1024:2048], qw[:, 1024:2048], AF.Abs,
                        accum_out=p.sm6[:, 6, 0:1]), "s6_a", 1)
                    self.wait("vector", "s6_a")
                    self.op("vector", lambda e: e.tensor_tensor(
                        p.sm6[:, 5, 0:8], p.sm6[:, 7, 0:8], p.sm6[:, 6, 0:8],
                        ALU.add), "vsp", 1)
                    self.wait("vector", "vsp")
                    self.op("vector", lambda e, NF=NF, trow=trow: e.tensor_scalar(
                        p.qsc[:, trow, 0:8], p.sm6[:, 5, 0:8], 0.5 / NF, None,
                        ALU.mult), "vsp", 1)
                self.wait("vector", "vsp")
                self.op("vector", lambda e, trow=trow: e.tensor_scalar(
                    p.qsc[:, trow, 0:8], p.qsc[:, trow, 0:8], 0.5e-5, None,
                    ALU.max), "vsp", 1)
                self.wait("vector", "vsp")
                self.op("vector", lambda e, trow=trow, nrow=nrow: e.tensor_scalar(
                    p.qsc[:, nrow, 0:8], p.qsc[:, trow, 0:8], -1.0, None,
                    ALU.mult), "v6_t", 1)
                self.wait("scalar", "v6_t")
                if dep.get("v6_q"):
                    self.wait("scalar", "v6_q", dep["v6_q"])
                self.op("scalar", lambda e, NF=NF, off=off, nrow=nrow, qw=qw: e.activation(
                    p.qo6[:, off:off + NF], qw[:, off:off + NF], AF.Sign,
                    bias=p.qsc[:, nrow, 0:1]), None)
                self.op("scalar", lambda e, NF=NF, off=off, trow=trow, qw=qw: e.activation(
                    p.lnbf[:, :, :].rearrange("q a f -> q (a f)")[:, off:off + NF],
                    qw[:, off:off + NF], AF.Sign, bias=p.qsc[:, trow, 0:1]), "s6_g", 1)
                self.wait("vector", "s6_g")
                self.op("vector", lambda e, NF=NF, off=off: e.tensor_tensor(
                    p.qo6[:, off:off + NF], p.qo6[:, off:off + NF],
                    p.lnbf[:, :, :].rearrange("q a f -> q (a f)")[:, off:off + NF],
                    ALU.add), "vsp", 1)
                self.wait("vector", "vsp")
                self.op("vector", lambda e, NF=NF, off=off, trow=trow: e.tensor_scalar(
                    p.qo6[:, off:off + NF], p.qo6[:, off:off + NF], p.qsc[:, trow, 0:1], None,
                    ALU.mult), "v6_q", 1)
                adv(j)
                self.wait("sync", "v6_q")
                dscr = p.w1q_scr if mat == 0 else p.w2q_scr
                so = "y6_qo" if j % 2 == 0 else "y6_qo2"
                self.op("sync", lambda e, t_i=t_i, NF=NF, off=off, epar=epar, dscr=dscr: e.dma_start(
                    dscr[epar, t_i * 128:(t_i + 1) * 128, :], p.qo6[:, off:off + NF]), so, 16)
                snap = {"y6_qo": self.gc("y6_qo"), "y6_qo2": self.gc("y6_qo2"),
                        "s6_g": self.gc("s6_g"), "v6_q": self.gc("v6_q")}
                for s in slots:
                    self.q_last[s] = snap
            self.wait("sync", "y6_qo")
            self.wait("sync", "y6_qo2")
            self.wait("sync", "t6_f1", self.gc("t6_f1"))
            self.wait("sync", "t6_f2", self.gc("t6_f2"))
            for dt_ in range(KD):
                self.op("sync", lambda e, dt_=dt_, epar=epar: e.dma_start(
                    p.w1T6[:, dt_, :], p.w1q_scr[epar][:, dt_ * 128:(dt_ + 1) * 128],
                    transpose=True), "y6_wt", 16)
            for ft in range(16):
                self.op("sync", lambda e, ft=ft, epar=epar: e.dma_start(
                    p.w2T6[:, ft, :], p.w2q_scr[epar][:, ft * 128:(ft + 1) * 128],
                    transpose=True), "y6_wt", 16)
            self.wait("tensor", "y6_wt")
            for c in range(4):
                hpar = 0
                self.wait("sync", "y6_h0")
                self.wait("sync", "t6_f1", self.gc("t6_f1"))
                self.op("sync", lambda e, c=c: e.dma_start(
                    p.h6st[:, 0, :, :],
                    p.h_scr.rearrange("(a q) f -> q a f", q=128)[:, :, c * 512:(c + 1) * 512]),
                    "y6_hc", 16)
                self.wait("tensor", "y6_hc")
                for ft in range(16):
                    g = self.gc("t6_f1")
                    gp = g % 2
                    if g >= 2:
                        self.wait("tensor", "s6_f1", g - 1)
                    for k in range(KD):
                        self.op("tensor", lambda e, ft=ft, k=k, gp=gp, hpar=hpar: e.matmul(
                            p.ps_a[:, gp, :], p.w1T6[:, k, ft * 128:(ft + 1) * 128],
                            p.h6st[:, hpar, k, :], start=(k == 0), stop=(k == KD - 1)),
                            "t6_f1" if k == KD - 1 else None, 1)
                    self.wait("scalar", "t6_f1")
                    if ft == 0:
                        self.wait("scalar", "v6_mg", self.gc("v6_mg"))
                        self.wait("scalar", "t6_f2", self.gc("t6_f2"))
                    self.op("scalar", lambda e, ft=ft, gp=gp: e.activation(
                        p.mid6[:, ft, :], p.ps_a[:, gp, :], AF.Silu), "s6_f1", 1)
                if ex < 8:
                    self.wait("tensor", "v6_st")
                    self.wait("tensor", "s6_gb", self.gc("s6_gb"))
                    gbp = c % 2
                    self.op("tensor", lambda e, ex=ex, c=c, gbp=gbp: e.matmul(
                        p.ps_c[:, gbp, :], p.sel8_sb[:, ex * 128:(ex + 1) * 128],
                        p.gateT[0:8, c * 512:(c + 1) * 512], start=True, stop=True), "t6_gb", 1)
                    self.wait("scalar", "t6_gb")
                    self.op("scalar", lambda e, gbp=gbp: e.activation(
                        p.gb6[:, :], p.ps_c[:, gbp, :], AF.Copy), "s6_gb", 1)
                    self.wait("vector", "s6_gb")
                    self.wait("vector", "s6_f1")
                    self.op("vector", lambda e: e.tensor_tensor(
                        p.mid6[:, :, :], p.mid6[:, :, :],
                        p.gb6[:, :].unsqueeze(1).broadcast_to([128, 16, 512]),
                        ALU.mult), "v6_mg", 1)
                    self.wait("tensor", "v6_mg")
                else:
                    self.wait("tensor", "s6_f1")
                for dt_ in range(8):
                    g = self.gc("t6_f2")
                    gp = g % 2
                    if g >= 2:
                        self.wait("tensor", "v6_f2", g - 1)
                    for kf in range(16):
                        self.op("tensor", lambda e, dt_=dt_, kf=kf, gp=gp: e.matmul(
                            p.ps_b[:, gp, :], p.w2T6[:, kf, dt_ * 128:(dt_ + 1) * 128],
                            p.mid6[:, kf, :], start=(kf == 0), stop=(kf == 15)),
                            "t6_f2" if kf == 15 else None, 1)
                    self.wait("vector", "t6_f2")
                    self.op("vector", lambda e, dt_=dt_, c=c, gp=gp: e.tensor_tensor(
                        p.stream[:, dt_, c * 512:(c + 1) * 512],
                        p.stream[:, dt_, c * 512:(c + 1) * 512], p.ps_b[:, gp, :],
                        ALU.add), "v6_f2", 1)


    def p7_store(self):
        p = self
        with contextlib.ExitStack() as sc:
            otm = self.sb("otm", [128, 2, D], F32, sc)
            self.wait("tensor", "v_ln", self.gc("v_ln"))
            for tt in range(16):
                par = tt % 2
                if tt >= 2:
                    self.wait("vector", "y_ot%d" % par, None)
                for dtl in range(8):
                    bp = (tt * 8 + dtl) % 2
                    if tt * 8 + dtl >= 2:
                        self.wait("tensor", "v_ot", self.gc("v_ot") - 1)
                    self.op("tensor", lambda e, tt=tt, dtl=dtl, bp=bp: e.transpose(
                        p.ps_b[:, bp, 0:128], p.stream[:, dtl, tt * 128:(tt + 1) * 128],
                        p.idnf_sb[:, :]), "t_ot", 1)
                    self.wait("vector", "t_ot")
                    self.op("vector", lambda e, tt=tt, dtl=dtl, par=par, bp=bp: e.tensor_copy(
                        otm[:, par, dtl * 128:(dtl + 1) * 128], p.ps_b[:, bp, 0:128]), "v_ot", 1)
                self.wait("sync", "v_ot")
                self.op("sync", lambda e, tt=tt, par=par: e.dma_start(
                    p.out[tt * 128:(tt + 1) * 128, :], otm[:, par, :]), "y_ot%d" % par, 16)
            self.wait("sync", "y_ot0")
            self.wait("sync", "y_ot1")


def build_program(stage=DEF_STAGE):
    nc = bass.Bass("TRN2", target_bir_lowering=False, detect_race_conditions=False)
    mk = MK(nc, stage)
    mk.build()
    return nc


def make_in_maps(inp, ncores=8):
    x = np.ascontiguousarray(inp["L0_attn_x"], dtype=np.float32)
    import ml_dtypes
    c = np.ascontiguousarray
    pm = lambda v, n: c(np.asarray(v, np.float32).reshape(n, 128).T)
    base = {
        "winT0": c(inp["L0_attn_Win0"].T), "bin0": pm(inp["L0_attn_bin0"], 24),
        "woutT0": c(inp["L0_attn_Wout0"].T), "bout0": pm(inp["L0_attn_bout0"], 8),
        "winT1": c(inp["L0_attn_Win1"].T), "bin1": pm(inp["L0_attn_bin1"], 24),
        "woutT1": c(inp["L0_attn_Wout1"].T), "bout1": pm(inp["L0_attn_bout1"], 8),
        "winT2": c(inp["L0_attn_Win2"].T), "bin2": pm(inp["L0_attn_bin2"], 24),
        "woutT2": c(inp["L0_attn_Wout2"].T), "bout2": pm(inp["L0_attn_bout2"], 8),
        "binv0": c(np.asarray(inp["L0_attn_bin0"], np.float32)[None, 2048:]),
        "binv1": c(np.asarray(inp["L0_attn_bin1"], np.float32)[None, 2048:]),
        "binv2": c(np.asarray(inp["L0_attn_bin2"], np.float32)[None, 2048:]),
        "ffg": pm(inp["L0_ffn_ln_gamma"], 8), "ffb": pm(inp["L0_ffn_ln_beta"], 8),
        "fc1q": c(inp["L0_ffn_fc1_q"]), "fc1s": c(inp["L0_ffn_fc1_scales"]),
        "fc1b": pm(inp["L0_ffn_fc1_bias"], 32), "fc2q": c(inp["L0_ffn_fc2_q"]),
        "fc2s": c(inp["L0_ffn_fc2_scales"]), "fc2b": pm(inp["L0_ffn_fc2_bias"], 8),
        "wq": c(inp["L1_attn_Wq"]), "bq": c(np.asarray(inp["L1_attn_bq"], np.float32)[None]),
        "wk": c(inp["L1_attn_Wk"]), "bk": c(np.asarray(inp["L1_attn_bk"], np.float32)[None]),
        "wv": c(inp["L1_attn_Wv"]), "bv": c(np.asarray(inp["L1_attn_bv"], np.float32)[None]),
        "wo": c(inp["L1_attn_Wo"]), "bo": pm(inp["L1_attn_bo"], 8),
        "ew1": c(np.concatenate([inp["L1_ffn_expert_w1"], inp["L1_ffn_shared_w1"][None]], 0)),
        "ew2": c(np.concatenate([inp["L1_ffn_expert_w2"], inp["L1_ffn_shared_w2"][None]], 0)),
        "rwT": c(inp["L1_ffn_router_w"].T), "g1": pm(inp["L1_ffn_ln_gamma"], 8),
        "b1": pm(inp["L1_ffn_ln_beta"], 8),
        "ones128": np.ones((1, 128), np.float32),
        "onescol_bf": np.ones((128, 1), ml_dtypes.bfloat16),
        "on1024_bf": np.full((128, 1), 1.0 / 1024.0, ml_dtypes.bfloat16),
        "idn_bf": np.eye(128, dtype=ml_dtypes.bfloat16),
        "idn_f": np.eye(128, dtype=np.float32),
        "sel8": np.concatenate([np.tile(np.eye(8, dtype=np.float32)[:, h:h + 1], (1, 128))
                                for h in range(8)], axis=1),
        "sel8h": np.concatenate([np.eye(8, dtype=ml_dtypes.bfloat16)[h][None, :].repeat(128, 0) * 0
                                 + np.eye(8, dtype=ml_dtypes.bfloat16)[:, h][None, :].astype(ml_dtypes.bfloat16) * 0
                                 + np.tile((np.arange(8) == h).astype(ml_dtypes.bfloat16)[None, :], (128, 1))
                                 for h in range(8)], axis=1).astype(ml_dtypes.bfloat16),
        "pw2": np.tile((2.0 ** -np.arange(8, dtype=np.float32))[None, :], (128, 1)),
    }
    maps = []
    for core in range(ncores):
        b, c2 = core // 2, core % 2
        m = dict(base)
        m["xwin"] = c(x[b, 448 * c2: 448 * c2 + WIN, :])
        m["xres"] = c(x[b, TC * c2: TC * (c2 + 1), :])
        m["mv"] = np.full((128, 1), float(c2), np.float32)
        m["mv1"] = np.full((128, 1), float(1 - c2), np.float32)
        maps.append(m)
    return maps


def kernel(**inp):
    nc = build_program()
    maps = make_in_maps(inp)
    res = run_bass_kernel_spmd(nc, maps, core_ids=list(range(len(maps))))
    out = np.empty((4, 4096, D), np.float32)
    for core in range(len(maps)):
        b, c2 = core // 2, core % 2
        out[b, TC * c2: TC * (c2 + 1), :] = res.results[core]["out"]
    return out


